# version 03
# baseline (speedup 1.0000x reference)
import numpy as np
import ml_dtypes
import concourse.bass as bass
import concourse.mybir as mybir

F32 = mybir.dt.float32
BF16 = mybir.dt.bfloat16
U8 = mybir.dt.uint8
U16 = mybir.dt.uint16
AF = mybir.ActivationFunctionType
ALU = mybir.AluOpType
AX = mybir.AxisListType
BF = ml_dtypes.bfloat16

EMBED = 2048
DIN = 4096
NH = 32
DSTATE = 64
CHUNK = 256
T = 2048
NCH = 8
CONVD = 4224
MLP = 16384
NCORE = 8
KT = 16
NCT = 66
TPAD = 2176


class S:
    def __init__(self, nc, name):
        self.h = nc.alloc_semaphore(name)
        self.n = 0

    def inc(self, instr, k=1):
        instr.then_inc(self.h, k)
        self.n += k
        return self.n


SEM_NAMES = [
    "ini", "cld", "uld", "wldA", "wslb", "xnw", "xnt", "ipe", "ipv", "zfw",
    "cvv", "sil", "xfw", "dtv", "mil", "afw", "arw", "aex", "xtl", "xpv",
    "pet", "stm", "sue", "suw", "scv", "ccw", "ccs", "gtl", "sin",
    "p2a", "p2b", "p2c", "p2d", "p2e", "p2f", "ope", "opv", "opw", "ffa", "ffb",
    "ffc", "ffd", "ffw", "ffg", "dqv", "dqw", "outw", "m2",
]


class Build:
    def __init__(self, taps=()):
        self.taps = set(taps)
        self.nc = bass.Bass("TRN2", target_bir_lowering=False)
        self.tap_list = []

    def declare_io(self):
        nc = self.nc
        self.u = nc.dram_tensor("u", [T, EMBED], F32, kind="ExternalInput")
        self.uh = nc.dram_tensor("uh", [3, EMBED], F32, kind="ExternalInput")
        self.ipw = nc.dram_tensor("ipw", [NCT * KT * 128, 128], F32, kind="ExternalInput")
        self.oww = nc.dram_tensor("oww", [DIN, EMBED], F32, kind="ExternalInput")
        self.convw = nc.dram_tensor("convw", [128, 33 * 4], F32, kind="ExternalInput")
        self.convb = nc.dram_tensor("convb", [128, 33], F32, kind="ExternalInput")
        self.alog = nc.dram_tensor("alog", [NH, 1], F32, kind="ExternalInput")
        self.dtb = nc.dram_tensor("dtb", [NH, 1], F32, kind="ExternalInput")
        self.dskip = nc.dram_tensor("dskip", [1, NH], F32, kind="ExternalInput")
        self.normw = nc.dram_tensor("normw", [DIN], F32, kind="ExternalInput")
        self.idbf = nc.dram_tensor("idbf", [128, 128], BF16, kind="ExternalInput")
        self.idf32 = nc.dram_tensor("idf32", [128, 128], F32, kind="ExternalInput")
        self.maskt = nc.dram_tensor("maskt", [2 * 128, 256], F32, kind="ExternalInput")
        self.ones32 = nc.dram_tensor("ones32", [NH, 256], F32, kind="ExternalInput")
        self.onesf = nc.dram_tensor("onesf", [1, 128], F32, kind="ExternalInput")
        self.sel9 = nc.dram_tensor("sel9", [9, 8], F32, kind="ExternalInput")
        self.constv = nc.dram_tensor("constv", [128, 4], F32, kind="ExternalInput")
        self.pkg = nc.dram_tensor("pkg", [1024, 2048], U8, kind="ExternalInput")
        self.pku = nc.dram_tensor("pku", [1024, 2048], U8, kind="ExternalInput")
        self.pkd = nc.dram_tensor("pkd", [1024, 2048], U8, kind="ExternalInput")
        self.scg = nc.dram_tensor("scg", [64, 2048], F32, kind="ExternalInput")
        self.scu = nc.dram_tensor("scu", [64, 2048], F32, kind="ExternalInput")
        self.scd = nc.dram_tensor("scd", [64, 2048], F32, kind="ExternalInput")
        self.out = nc.dram_tensor("outp", [T, EMBED], F32, kind="ExternalOutput")
        self.xn_tm = nc.dram_tensor("xn_tm", [TPAD, EMBED], BF16, kind="Internal")
        self.ipw16 = nc.dram_tensor("ipw16", [NCT * KT * 128, 128], BF16, kind="Internal")
        self.ow16 = nc.dram_tensor("ow16", [DIN, EMBED], BF16, kind="Internal")
        self.x_fm = nc.dram_tensor("x_fm", [DIN, T], BF16, kind="Internal")
        self.z_fm = nc.dram_tensor("z_fm", [DIN, T], BF16, kind="Internal")
        self.su_dram = nc.dram_tensor("su_dram", [NCH * DSTATE, DIN], F32, kind="Internal")
        self.aflat = nc.dram_tensor("aflat", [NCH, NH * CHUNK], F32, kind="Internal")
        self.asd = nc.dram_tensor("asd", [1, NH], F32, kind="Internal")
        self.decd = nc.dram_tensor("decd", [1, 256], F32, kind="Internal")
        self.yn_tm = nc.dram_tensor("yn_tm", [T, DIN], BF16, kind="Internal")
        self.attn = nc.dram_tensor("attn", [T, EMBED], F32, kind="Internal")
        self.cinS = nc.dram_tensor("cinS", [DSTATE + 1, DIN], F32, kind="Internal")
        self.gathS = nc.dram_tensor("gathS", [NCORE * (DSTATE + 1), DIN], F32, kind="Internal")
        self.cinW = nc.dram_tensor("cinW", [3 * 2048, 2048], BF16, kind="Internal")
        self.cinW2 = nc.dram_tensor("cinW2", [3 * 2048, 2048], BF16, kind="Internal")
        self.gathW = nc.dram_tensor("gathW", [NCORE * 3 * 2048, 2048], BF16, kind="Internal")

    def alloc_sbuf(self):
        nc = self.nc
        A = nc.alloc_sbuf_tensor
        self.c_idb = A("c_idb", [128, 128], BF16)
        self.c_idf = A("c_idf", [128, 128], F32)
        self.c_mask = A("c_mask", [128, 2 * 256], F32)
        self.c_convw = A("c_convw", [128, 33 * 4], F32)
        self.c_convb = A("c_convb", [128, 33], F32)
        self.c_normw = A("c_normw", [128, 32], F32)
        self.c_drep = A("c_drep", [128, NH], F32)
        self.c_ones32 = A("c_ones32", [NH, 256], F32)
        self.c_onesf = A("c_onesf", [1, 128], F32)
        self.c_sel9 = A("c_sel9", [9, 8], F32)
        self.c_acol = A("c_acol", [NH, 1], F32)
        self.c_dtb = A("c_dtb", [NH, 1], F32)
        self.c_init = A("c_init", [128, 16], F32)
        self.c_cst = A("c_cst", [128, 4], F32)
        self.k32 = A("k32", [NH, 8720], F32)
        self.k_a_fm = self.k32[:, 0:2048]
        self.k_dtpos = self.k32[:, 2048:4096]
        self.k_dA = self.k32[:, 4096:5120]
        self.k_d2e = self.k32[:, 5120:5376]
        self.k_de2f = self.k32[:, 5376:5632]
        self.k_t1 = self.k32[:, 5632:6656]
        self.k_t2 = self.k32[:, 6656:7680]
        self.k_dtraw = self.k32[:, 7680:8707]
        self.c_asum = self.k32[:, 8707:8708]
        self.k_atm = A("k_atm", [128, NCH * 64], F32)
        self.k_etm = A("k_etm", [128, NCH * 64], F32)
        self.k_dttm = A("k_dttm", [128, NCH * 64], F32)
        self.k_de2 = A("k_de2", [128, 64], F32)
        self.k_btm = A("k_btm", [128, 128], BF16)
        self.k_bc = A("k_bc", [128, NCH * CHUNK], BF16)
        self.s_pair = A("s_pair", [128, DIN], F32)
        self.s_loc = self.s_pair[0:64, :]
        self.s_su = self.s_pair[64:128, :]
        self.s_b16 = A("s_b16", [128, DIN], BF16)
        self.s_p16 = self.s_b16[0:64, :]
        self.k_elr = A("k_elr", [64, NCH * NH], F32)
        self.AB = 136064
        self.arena = A("arena", [128, self.AB], U8)
        self._aoff = 0


    def seg(self, nbytes, dtype):
        nbytes = (nbytes + 31) & ~31
        t = self.arena[:, self._aoff:self._aoff + nbytes]
        self._aoff += nbytes
        assert self._aoff <= self.AB, f"arena overflow {self._aoff}"
        return t.bitcast(dtype)

    def layout_p01(self):
        self._aoff = 0
        r1_base = self._aoff
        self.a_xbc = [self.seg(1028 * 2, BF16) for _ in range(33)]
        self.a_ubuf = self.arena[:, r1_base:r1_base + 8192].bitcast(F32)
        self.a_sq = self.arena[:, r1_base + 8192:r1_base + 16384].bitcast(F32)
        self.a_xtm = self.arena[:, r1_base:r1_base + 16384].bitcast(BF16)
        self.a_xn0 = self.arena[:, r1_base + 16384:r1_base + 20480].bitcast(BF16)
        self.a_xpp = self.arena[:, r1_base + 16384:r1_base + 32768].bitcast(BF16)
        self.a_xnt = [self.seg(1152 * 2, BF16) for _ in range(KT)]
        self.a_wslab = self.seg(2 * 2048 * 2, BF16)
        self.a_z16 = self.seg(1024 * 2, BF16)
        self.a_cv16a = self.seg(1024 * 2, BF16)
        self.a_conv = self.seg(1024 * 4, F32)
        self.a_small = self.seg(640 * 4, F32)
        self.dq_pk = [self.seg(512, U8), self.seg(512, U8)]
        self.dq_pk16 = [p.bitcast(U16) for p in self.dq_pk]
        self.dq_scx = self.seg(1024 * 2, BF16)
        self.dq_lo = self.seg(512, U8)
        self.dq_hi = self.seg(512, U8)
        self.dq_lo16 = self.dq_lo.bitcast(U16)
        self.dq_hi16 = self.dq_hi.bitcast(U16)
        self.dq_c = self.seg(512 * 2, BF16)
        self.dq_ge = self.seg(512 * 2, BF16)
        self.dq_r = self.seg(512 * 2, BF16)
        self.dq_t = self.seg(512 * 2, BF16)
        self.dq_s = self.seg(512 * 2, BF16)
        self.dq_o = [self.seg(512 * 2, BF16) for _ in range(2)]
        self.p01_end = self._aoff
        assert self._aoff <= self.AB, self._aoff


    def alloc_psum(self):
        self.pb = [self.nc.alloc_psum_tensor(f"pb{i}", [128, 512], F32) for i in range(8)]

    def make_sems(self):
        self.S = {n: S(self.nc, n) for n in SEM_NAMES}

    def add_tap_dram(self, name, src_ap, shape, dtype):
        if name not in self.taps:
            return
        t = self.nc.dram_tensor("tap_" + name, shape, dtype, kind="ExternalOutput")
        self.tap_list.append((t, src_ap))

    def flush_taps(self, blk, wait_sems):
        if not self.tap_list and "outp" not in self.taps:
            pass

        @blk.gpsimd
        def _(g):
            for sem, val in wait_sems:
                g.wait_ge(sem.h, val)
            st = self.S["outw"]
            for t, src in self.tap_list:
                st.inc(g.dma_start(t[:], src), 16)
            g.wait_ge(st.h, st.n)

    def p0(self, blk):
        nc = self.nc
        Ss = self.S

        @blk.gpsimd
        def _(g):
            g.memset(self.c_init[:, 0:16], 0.0)
            g.memset(self.c_init[:, 0:16], 0.0)
            Ss["ini"].inc(g.nop(), 1)
            ct_order = list(range(32, 66)) + list(range(32))
            pieces = [ct_order[0:17], ct_order[17:34], ct_order[34:50], ct_order[50:66]]
            self.cast_need = {}
            for pi, piece in enumerate(pieces):
                for ct in piece:
                    self.cast_need[ct] = 16 * (pi + 1)
                runs = []
                for ct in piece:
                    if runs and runs[-1][1] == ct:
                        runs[-1] = (runs[-1][0], ct + 1)
                    else:
                        runs.append((ct, ct + 1))
                last = None
                for a, bb in runs:
                    r0_, r1_ = a * KT * 128, bb * KT * 128
                    last = g.dma_start(self.ipw16[r0_:r1_, :], self.ipw[r0_:r1_, :])
                Ss["wldA"].inc(last, 16)

        @blk.sync
        def _(sync):
            ld = Ss["cld"]
            ld.inc(sync.dma_start(self.c_idb[:], self.idbf[:]), 16)
            ld.inc(sync.dma_start(self.c_idf[:], self.idf32[:]), 16)
            ld.inc(sync.dma_start(self.c_mask[:, 0:256], self.maskt[0:128, :]), 16)
            ld.inc(sync.dma_start(self.c_mask[:, 256:512], self.maskt[128:256, :]), 16)
            ld.inc(sync.dma_start(self.c_convw[:], self.convw[:]), 16)
            ld.inc(sync.dma_start(self.c_convb[:], self.convb[:]), 16)
            for i in range(32):
                ld.inc(sync.dma_start(self.c_normw[:, i:i + 1], self.normw[128 * i:128 * (i + 1), None]), 16)
            ld.inc(sync.dma_start(self.c_drep[:], self.dskip[0, None, :].broadcast_to((128, NH))), 16)
            ld.inc(sync.dma_start(self.c_ones32[:], self.ones32[:]), 16)
            ld.inc(sync.dma_start(self.c_onesf[:], self.onesf[:]), 16)
            ld.inc(sync.dma_start(self.c_sel9[:], self.sel9[:]), 16)
            ld.inc(sync.dma_start(self.c_acol[:], self.alog[:]), 16)
            ld.inc(sync.dma_start(self.c_dtb[:], self.dtb[:]), 16)
            ld.inc(sync.dma_start(self.c_cst[:], self.constv[:]), 16)

        @blk.scalar
        def _(s):
            s.wait_ge(Ss["ini"].h, 1)
            s.wait_ge(Ss["cld"].h, Ss["cld"].n)
            Ss["aex"].inc(s.activation(self.c_acol[:], self.c_acol[:], AF.Exp), 1)

        @blk.sync
        def _(sync):
            for tt in range(17):
                sync.wait_ge(Ss["cvv"].h, tt)
                if tt == 0:
                    Ss["uld"].inc(sync.dma_start(self.a_ubuf[0:3, :], self.uh[:]), 16)
                else:
                    r0 = 128 * (tt - 1)
                    Ss["uld"].inc(sync.dma_start(self.a_ubuf[:], self.u[r0:r0 + 128, :]), 16)

        A1 = 0

        @blk.vector
        def _(v):
            v.wait_ge(Ss["ini"].h, 1)
            sm = self.a_small
            m, s1, s2, q, var, sd, rstd = (sm[:, i:i + 1] for i in range(7))
            for tt in range(17):
                P = 3 if tt == 0 else 128
                ub = self.a_ubuf[0:P, :]
                v.wait_ge(Ss["uld"].h, 16 * (tt + 1))
                v.reduce_sum(s1[0:P, :], ub, axis=AX.X)
                v.wait_ge(Ss["sil"].h, A1 + 2 * tt + 1)
                Ss["mil"].inc(v.reduce_sum(s2[0:P, :], self.a_sq[0:P, :], axis=AX.X), 1)
                v.wait_ge(Ss["mil"].h, 4 * tt + 1)
                v.tensor_scalar(m[0:P, :], s1[0:P, :], 1.0 / EMBED, None, ALU.mult)
                Ss["mil"].inc(v.tensor_scalar(q[0:P, :], s2[0:P, :], 1.0 / EMBED, None, ALU.mult), 2)
                v.wait_ge(Ss["mil"].h, 4 * tt + 3)
                Ss["mil"].inc(v.tensor_tensor(var[0:P, :], m[0:P, :], m[0:P, :], ALU.mult), 1)
                v.wait_ge(Ss["mil"].h, 4 * tt + 4)
                Ss["dtv"].inc(v.tensor_tensor(var[0:P, :], q[0:P, :], var[0:P, :], ALU.subtract), 1)
                v.wait_ge(Ss["sil"].h, A1 + 2 * tt + 2)
                Ss["dtv"].inc(v.reciprocal(rstd[0:P, :], sd[0:P, :]), 1)
                v.wait_ge(Ss["dtv"].h, 2 * tt + 2)
                v.wait_ge(Ss["xnw"].h, 16 * tt)
                wr = v.tensor_scalar(self.a_xn0[0:P, 0:2048], ub, m[0:P, 0:1], rstd[0:P, 0:1],
                                     ALU.subtract, ALU.mult)
                Ss["cvv"].inc(wr, 1)

        @blk.scalar
        def _(s):
            sm = self.a_small
            for tt in range(17):
                P = 3 if tt == 0 else 128
                s.wait_ge(Ss["uld"].h, 16 * (tt + 1))
                s.wait_ge(Ss["mil"].h, 4 * tt)
                Ss["sil"].inc(s.activation(self.a_sq[0:P, :], self.a_ubuf[0:P, :], AF.Square), 1)
                s.wait_ge(Ss["dtv"].h, 2 * tt + 1)
                Ss["sil"].inc(s.activation(sm[0:P, 5:6], sm[0:P, 4:5], AF.Sqrt, bias=self.c_cst[0:P, 0:1]), 1)

        @blk.gpsimd
        def _(g):
            for tt in range(17):
                P = 3 if tt == 0 else 128
                r0 = 0 if tt == 0 else 3 + 128 * (tt - 1)
                g.wait_ge(Ss["cvv"].h, tt + 1)
                Ss["xnw"].inc(g.dma_start(self.xn_tm[r0:r0 + P, :], self.a_xn0[0:P, 0:2048]), 16)
            g.wait_ge(Ss["xnw"].h, 17 * 16)
            Ss["xnw"].inc(g.nop(), 1)

        self.ln_cvv = Ss["cvv"].n
        self.ln_dtv = Ss["dtv"].n
        self.ln_sil = Ss["sil"].n
        self.ln_mil = Ss["mil"].n

    def p1_half(self, blk, h):
        nc = self.nc
        Ss = self.S
        pb = self.pb
        order = list(range(32, 66)) + list(range(32))
        r0 = 1024 * h
        b_xnt = Ss["xnt"].n
        b_wsl = Ss["wslb"].n
        b_ipe = Ss["ipe"].n
        b_ipv = Ss["ipv"].n
        b_zfw = Ss["zfw"].n
        b_cvv = Ss["cvv"].n
        b_sil = Ss["sil"].n
        b_xfw = Ss["xfw"].n

        @blk.sync
        def _(sync):
            if h == 0:
                sync.wait_ge(Ss["xnw"].h, 16 * 10)
            else:
                sync.wait_ge(Ss["xnw"].h, 17 * 16)
                sync.wait_ge(Ss["ipe"].h, b_ipe)
            for k in range(KT):
                Ss["xnt"].inc(sync.dma_start_transpose(
                    self.a_xnt[k][:, 0:1152], self.xn_tm[r0:r0 + 1152, 128 * k:128 * (k + 1)]), 16)
            for idx, ct in enumerate(order):
                if h == 0:
                    sync.wait_ge(Ss["wldA"].h, self.cast_need[ct])
                if idx >= 2:
                    sync.wait_ge(Ss["ipe"].h, b_ipe + idx - 1)
                par = idx % 2
                rows0 = ct * KT * 128
                Ss["wslb"].inc(sync.dma_start(
                    self.a_wslab[:, 2048 * par:2048 * (par + 1)].rearrange("p (k n) -> p k n", k=KT),
                    self.ipw16[rows0:rows0 + KT * 128, :].rearrange("(k p) n -> p k n", p=128)), 16)

        @blk.tensor
        def _(t):
            t.wait_ge(Ss["xnt"].h, b_xnt + 16 * KT)
            for idx, ct in enumerate(order):
                t.wait_ge(Ss["wslb"].h, b_wsl + 16 * (idx + 1))
                par = idx % 2
                tri = pb[0:3] if par == 0 else pb[3:6]
                if idx >= 2:
                    t.wait_ge(Ss["ipv"].h, b_ipv + idx - 1)
                elif h == 1:
                    t.wait_ge(Ss["ipv"].h, b_ipv - 2 + idx + 1)
                for k in range(KT):
                    wk = self.a_wslab[:, 2048 * par + 128 * k: 2048 * par + 128 * (k + 1)]
                    t.ldweights(wk)
                    st_, sp = (k == 0), (k == KT - 1)
                    xk = self.a_xnt[k]
                    t.matmul(tri[0][:, 0:512], wk, xk[:, 0:512], start=st_, stop=sp)
                    t.matmul(tri[1][:, 0:512], wk, xk[:, 512:1024], start=st_, stop=sp)
                    mm = t.matmul(tri[2][:, 0:3], wk, xk[:, 1024:1027], start=st_, stop=sp)
                    if sp:
                        Ss["ipe"].inc(mm, 1)

        @blk.scalar
        def _(s):
            for idx, ct in enumerate(order):
                s.wait_ge(Ss["ipe"].h, b_ipe + idx + 1)
                par = idx % 2
                tri = pb[0:3] if par == 0 else pb[3:6]
                if ct == 65:
                    dst = self.k_dtraw
                    s.copy(dst[:, 0:512], tri[0][0:32, 0:512])
                    s.copy(dst[:, 512:1024], tri[1][0:32, 0:512])
                    cp = s.copy(dst[:, 1024:1027], tri[2][0:32, 0:3])
                elif ct >= 32:
                    dst = self.a_xbc[ct - 32]
                    s.copy(dst[:, 0:512], tri[0][:, 0:512])
                    s.copy(dst[:, 512:1024], tri[1][:, 0:512])
                    cp = s.copy(dst[:, 1024:1027], tri[2][:, 0:3])
                else:
                    zi = idx - 34
                    buf = self.a_z16
                    if zi >= 1:
                        s.wait_ge(Ss["zfw"].h, b_zfw + 16 * zi)
                    s.copy(buf[:, 0:509], tri[0][:, 3:512])
                    s.copy(buf[:, 509:1021], tri[1][:, 0:512])
                    cp = s.copy(buf[:, 1021:1024], tri[2][:, 0:3])
                    Ss["ipv"].inc(cp, 1)
                    s.wait_ge(Ss["ipv"].h, b_ipv + idx + 1)
                    Ss["zfw"].inc(s.dma_start(
                        self.z_fm[128 * zi:128 * (zi + 1), r0:r0 + 1024], buf[:, 0:1024]), 16)
                    continue
                Ss["ipv"].inc(cp, 1)

        @blk.vector
        def _(v):
            v.wait_ge(Ss["cld"].h, Ss["cld"].n)
            for c in range(33):
                v.wait_ge(Ss["ipv"].h, b_ipv + c + 1)
                if c >= 1:
                    v.wait_ge(Ss["sil"].h, b_sil + c)
                xb = self.a_xbc[c]
                w = self.c_convw
                cv = self.a_conv[:, 0:1024]
                v.tensor_scalar(cv, xb[:, 0:1024], w[:, 4 * c:4 * c + 1], None, ALU.mult)
                v.scalar_tensor_tensor(cv, xb[:, 1:1025], w[:, 4 * c + 1:4 * c + 2], cv, ALU.mult, ALU.add)
                v.scalar_tensor_tensor(cv, xb[:, 2:1026], w[:, 4 * c + 2:4 * c + 3], cv, ALU.mult, ALU.add)
                cc_ = v.scalar_tensor_tensor(cv, xb[:, 3:1027], w[:, 4 * c + 3:4 * c + 4], cv, ALU.mult, ALU.add)
                Ss["cvv"].inc(cc_, 1)

        @blk.scalar
        def _(s):
            for c in range(33):
                s.wait_ge(Ss["cvv"].h, b_cvv + c + 1)
                if c < 32:
                    if c >= 1:
                        s.wait_ge(Ss["xfw"].h, b_xfw + 16 * c)
                    ac = s.activation(self.a_cv16a[:, 0:1024], self.a_conv[:, 0:1024], AF.Silu,
                                      bias=self.c_convb[:, c:c + 1])
                    Ss["sil"].inc(ac, 1)
                    s.wait_ge(Ss["sil"].h, b_sil + c + 1)
                    Ss["xfw"].inc(s.dma_start(
                        self.x_fm[128 * c:128 * (c + 1), r0:r0 + 1024], self.a_cv16a[:, 0:1024]), 16)
                else:
                    ac = s.activation(self.k_bc[:, 1024 * h:1024 * (h + 1)], self.a_conv[:, 0:1024],
                                      AF.Silu, bias=self.c_convb[:, c:c + 1])
                    Ss["sil"].inc(ac, 1)

        self.conv_sil_done[h] = Ss["sil"].n
        if not hasattr(self, "ipv_half_done"):
            self.ipv_half_done = {}
        self.ipv_half_done[h] = b_ipv + 66

        @blk.scalar
        def _(s):
            t1, t2 = self.k_t1, self.k_t2
            dtr = self.k_dtraw[:, 3:1027]
            s.wait_ge(Ss["ipv"].h, b_ipv + 34)
            s.activation(t1, dtr, AF.Abs, bias=self.c_dtb[:, 0:1])
            s.activation(t1, t1, AF.Exp, scale=-1.0)
            s.activation(t1, t1, AF.Ln, bias=1.0)
            ac = s.activation(t2, dtr, AF.Relu, bias=self.c_dtb[:, 0:1])
            Ss["sil"].inc(ac, 1)

        sil_dt = Ss["sil"].n

        @blk.vector
        def _(v):
            dA = self.k_dA
            v.wait_ge(Ss["sil"].h, sil_dt)
            v.wait_ge(Ss["aex"].h, 1)
            dtp = self.k_dtpos[:, r0:r0 + 1024]
            v.tensor_tensor(dtp, self.k_t1, self.k_t2, ALU.add)
            da_ = v.tensor_scalar(dA, dtp, self.c_acol[0:32, 0:1], -1.0, ALU.mult, ALU.mult)
            Ss["dtv"].inc(da_, 1)
            v.wait_ge(Ss["dtv"].h, Ss["dtv"].n)
            for lc in range(4):
                cc = 4 * h + lc
                sc_ = v.tensor_tensor_scan(
                    self.k_a_fm[:, 256 * cc:256 * (cc + 1)], self.c_ones32[:, 0:256],
                    dA[:, 256 * lc:256 * (lc + 1)], 0.0, ALU.mult, ALU.add)
                Ss["dtv"].inc(sc_, 1)
            v.wait_ge(Ss["dtv"].h, Ss["dtv"].n)
            for lc in range(4):
                cc = 4 * h + lc
                alast = self.k_a_fm[:, 256 * cc + 255:256 * cc + 256]
                if cc == 0:
                    Ss["dtv"].inc(v.tensor_copy(self.c_asum, alast), 1)
                else:
                    Ss["dtv"].inc(v.tensor_tensor(self.c_asum, self.c_asum, alast, ALU.add), 1)
                v.wait_ge(Ss["dtv"].h, Ss["dtv"].n)
            Ss["mil"].inc(v.nop(), 1)

        for lc in range(4):
            self.p1_one_chunk(blk, h, 4 * h + lc)

    def p1_one_chunk(self, blk, h, cc):
        nc = self.nc
        Ss = self.S
        pb = self.pb
        afm = self.k_a_fm[:, 256 * cc:256 * (cc + 1)]
        dtp = self.k_dtpos[:, 256 * cc:256 * (cc + 1)]
        d2e, de2f = self.k_d2e, self.k_de2f
        arow = self.a_small[0:1, 512:544]
        erow = self.a_small[0:1, 544:576]
        mile = Ss["mil"].n

        b_afw = Ss["afw"].n
        b_arw = Ss["arw"].n
        b_aex = Ss["aex"].n
        b_pet = Ss["pet"].n
        b_scv = Ss["scv"].n
        b_xtl = Ss["xtl"].n
        b_xpv = Ss["xpv"].n
        b_stm = Ss["stm"].n
        b_sue = Ss["sue"].n
        b_suw = Ss["suw"].n

        @blk.gpsimd
        def _(g):
            g.wait_ge(Ss["mil"].h, mile)
            Ss["afw"].inc(g.dma_start(
                self.aflat[cc, :].rearrange("(hh t) -> hh t", hh=NH), afm), 16)

        @blk.sync
        def _(sync):
            sync.wait_ge(Ss["afw"].h, b_afw + 16)
            sync.wait_ge(Ss["aex"].h, b_aex)
            Ss["arw"].inc(sync.dma_start(
                arow,
                self.aflat[cc, :].rearrange("(hh t) -> hh t", hh=NH)[:, 255][None, :],
                allow_slow_non_contiguous=True), 16)
            sync.wait_ge(Ss["xfw"].h, 16 * 32 * (h + 1))
            sync.wait_ge(Ss["stm"].h, b_stm)
            sync.wait_ge(Ss["xpv"].h, b_xpv)
            for tt in range(2):
                c0 = 256 * cc + 128 * tt
                Ss["xtl"].inc(sync.dma_start_transpose(
                    self.a_xtm[:, 4096 * tt:4096 * (tt + 1)], self.x_fm[:, c0:c0 + 128]), 16)

        @blk.scalar
        def _(s):
            s.wait_ge(Ss["mil"].h, mile)
            s.wait_ge(Ss["scv"].h, b_scv)
            Ss["aex"].inc(s.activation(d2e, afm, AF.Exp, scale=-1.0, bias=afm[:, 255:256]), 1)
            s.wait_ge(Ss["arw"].h, b_arw + 16)
            s.wait_ge(Ss["pet"].h, b_pet)
            Ss["aex"].inc(s.activation(erow, arow, AF.Exp), 1)

        @blk.vector
        def _(v):
            v.wait_ge(Ss["aex"].h, b_aex + 1)
            v.wait_ge(Ss["pet"].h, b_pet)
            Ss["scv"].inc(v.tensor_tensor(de2f, dtp, d2e, ALU.mult), 1)

        tsrc = [afm[:, 0:128], afm[:, 128:256],
                dtp[:, 0:128], dtp[:, 128:256],
                de2f[:, 0:128], de2f[:, 128:256]]

        @blk.tensor
        def _(t):
            t.wait_ge(Ss["mil"].h, mile)
            for i in range(6):
                bank = pb[6 + i % 2]
                if i >= 4:
                    t.wait_ge(Ss["scv"].h, b_scv + 1)
                if i >= 2:
                    t.wait_ge(Ss["sue"].h, b_sue + i - 1)
                tr = t.transpose(bank[:, 0:32], tsrc[i], self.c_idf[0:32, 0:32])
                Ss["pet"].inc(tr, 1)
            t.wait_ge(Ss["aex"].h, b_aex + 2)
            t.wait_ge(Ss["sue"].h, b_sue + 5)
            mmr = t.matmul(pb[6][0:64, 256:288], self.c_onesf[0:1, 0:64], erow, start=True, stop=True)
            Ss["pet"].inc(mmr, 1)
            ptb = pb[7][:, 32:64].bitcast(BF16)
            t.wait_ge(Ss["sil"].h, self.conv_sil_done[h])
            t.wait_ge(Ss["sue"].h, b_sue + 6)
            tr = t.transpose(ptb, self.k_bc[0:64, 256 * cc:256 * cc + 128], self.c_idb[0:64, 0:64])
            Ss["pet"].inc(tr, 1)
            t.wait_ge(Ss["sue"].h, b_sue + 8)
            tr = t.transpose(ptb, self.k_bc[0:64, 256 * cc + 128:256 * cc + 256], self.c_idb[0:64, 0:64])
            Ss["pet"].inc(tr, 1)
            t.wait_ge(Ss["ipv"].h, self.ipv_half_done[h])
            t.wait_ge(Ss["xpv"].h, b_xpv + 1)
            t.wait_ge(Ss["sue"].h, b_sue + 9)
            for g8 in range(8):
                bank = pb[4] if g8 % 2 == 0 else pb[5]
                if g8 >= 2:
                    t.wait_ge(Ss["sue"].h, b_sue + 8 + g8)
                for tt in range(2):
                    mm = t.matmul(bank[0:64, 0:512],
                                  self.k_btm[:, 64 * tt:64 * (tt + 1)],
                                  self.a_xpp[:, 4096 * tt + 512 * g8:4096 * tt + 512 * (g8 + 1)],
                                  start=(tt == 0), stop=(tt == 1))
                Ss["stm"].inc(mm, 1)

        @blk.vector
        def _(v):
            dst6 = [self.k_atm[:, 64 * cc:64 * cc + 32], self.k_atm[:, 64 * cc + 32:64 * cc + 64],
                    self.k_dttm[:, 64 * cc:64 * cc + 32], self.k_dttm[:, 64 * cc + 32:64 * cc + 64],
                    self.k_de2[:, 0:32], self.k_de2[:, 32:64]]
            for i in range(6):
                bank = pb[6 + i % 2]
                v.wait_ge(Ss["pet"].h, b_pet + i + 1)
                Ss["sue"].inc(v.tensor_copy(dst6[i], bank[:, 0:32]), 1)
            v.wait_ge(Ss["pet"].h, b_pet + 7)
            Ss["sue"].inc(v.tensor_copy(self.k_elr[:, 32 * cc:32 * (cc + 1)], pb[6][0:64, 256:288]), 1)
            ptb = pb[7][:, 32:64].bitcast(BF16)
            v.wait_ge(Ss["pet"].h, b_pet + 8)
            Ss["sue"].inc(v.tensor_copy(self.k_btm[:, 0:64], ptb), 1)
            v.wait_ge(Ss["pet"].h, b_pet + 9)
            Ss["sue"].inc(v.tensor_copy(self.k_btm[:, 64:128], ptb), 1)
            v.wait_ge(Ss["xtl"].h, b_xtl + 32)
            v.wait_ge(Ss["sue"].h, b_sue + 9)
            op = None
            for tt in range(2):
                for hh in range(NH):
                    sl = slice(4096 * tt + 128 * hh, 4096 * tt + 128 * (hh + 1))
                    op = v.tensor_scalar(self.a_xpp[:, sl], self.a_xtm[:, sl],
                                         self.k_de2[:, 32 * tt + hh:32 * tt + hh + 1], None, ALU.mult)
            Ss["xpv"].inc(op, 1)
            for g8 in range(8):
                bank = pb[4] if g8 % 2 == 0 else pb[5]
                v.wait_ge(Ss["stm"].h, b_stm + g8 + 1)
                if g8 == 0:
                    v.wait_ge(Ss["suw"].h, b_suw)
                v.tensor_copy(self.s_su[:, 512 * g8:512 * (g8 + 1)], bank[0:64, 0:512])
                if cc == 0:
                    cp = v.tensor_copy(self.s_loc[:, 512 * g8:512 * (g8 + 1)], bank[0:64, 0:512])
                else:
                    cp = v.scalar_tensor_tensor(self.s_loc[:, 512 * g8:512 * (g8 + 1)],
                                                bank[0:64, 0:512], 1.0,
                                                self.s_loc[:, 512 * g8:512 * (g8 + 1)],
                                                ALU.mult, ALU.add)
                Ss["sue"].inc(cp, 1)
            v.wait_ge(Ss["sue"].h, b_sue + 17)
            op = None
            for hh in range(NH):
                op = v.tensor_scalar(self.s_loc[:, 128 * hh:128 * (hh + 1)],
                                     self.s_loc[:, 128 * hh:128 * (hh + 1)],
                                     self.k_elr[:, 32 * cc + hh:32 * cc + hh + 1], None, ALU.mult)
            Ss["scv"].inc(op, 1)

        @blk.scalar
        def _(s):
            s.wait_ge(Ss["sue"].h, b_sue + 2)
            Ss["aex"].inc(s.activation(self.k_etm[:, 64 * cc:64 * (cc + 1)],
                                       self.k_atm[:, 64 * cc:64 * (cc + 1)], AF.Exp), 1)

        @blk.gpsimd
        def _(g):
            g.wait_ge(Ss["sue"].h, b_sue + 17)
            Ss["suw"].inc(g.dma_start(self.su_dram[64 * cc:64 * (cc + 1), :], self.s_su), 16)

    def layout_p2(self):
        r1 = 0
        self.q_xtm = self.arena[:, r1:r1 + 16384].bitcast(BF16)
        self.q_xp = self.arena[:, r1 + 16384:r1 + 32768].bitcast(BF16)
        self.q_y0 = self.arena[:, r1 + 32768:r1 + 49152].bitcast(BF16)
        self.q_zyn = self.arena[:, r1 + 49152:r1 + 65536].bitcast(BF16)
        r2 = 33 * 2056 // 32 * 32
        r2 = (33 * 2056 + 31) & ~31
        self.q_sig = self.arena[:, r2:r2 + 16384].bitcast(F32)
        self.q_su2 = self.arena[:, r2 + 16384:r2 + 32768].bitcast(F32)[0:64, :]
        r3 = r2 + 36864
        self.q_mt = self.arena[:, r3:r3 + 4096].bitcast(BF16)
        self.q_sct = self.arena[:, r3 + 4096:r3 + 5120].bitcast(BF16)
        self.q_seg = self.arena[:, r3 + 5120:r3 + 9216].bitcast(F32)
        self.q_es = self.arena[:, r3 + 9216:r3 + 11264].bitcast(BF16)
        self.q_dec = self.arena[:, r3 + 11264:r3 + 12288].bitcast(F32)[0:64, :]
        self.q_as9 = self.arena[:, r3 + 12288:r3 + 12416].bitcast(F32)[0:9, 0:32]
        self.q_ccp = self.arena[:, r3 + 12416:r3 + 12928].bitcast(BF16)[0:64, :]
        self.q_arow = self.k32[0:1, 0:8192]

    def p2_pre(self, blk):
        nc = self.nc
        Ss = self.S
        pb = self.pb
        b_gtl = Ss["gtl"].n
        scv_all = Ss["scv"].n
        suw_all = Ss["suw"].n

        @blk.gpsimd
        def _(g):
            g.wait_ge(Ss["scv"].h, scv_all)
            g.wait_ge(Ss["suw"].h, suw_all)
            Ss["gtl"].inc(g.dma_start(self.cinS[0:64, :], self.s_loc), 16)
            Ss["gtl"].inc(g.dma_start(self.asd[0, :][:, None], self.c_asum), 16)
            g.wait_ge(Ss["gtl"].h, b_gtl + 32)
            Ss["gtl"].inc(g.dma_start(self.cinS[64:65, 0:NH], self.asd[:]), 16)
            g.wait_ge(Ss["gtl"].h, b_gtl + 48)
            g.collective_compute(
                "AllGather", mybir.AluOpType.bypass,
                replica_groups=[list(range(NCORE))],
                ins=[self.cinS.ap().opt()],
                outs=[self.gathS.ap().opt()],
            ).then_inc(Ss["ccs"].h, 1)
            Ss["ccs"].n += 1
            g.wait_ge(Ss["ccs"].h, 1)
            for j in range(NCORE):
                Ss["gtl"].inc(g.dma_start(self.q_as9[j:j + 1, :], self.gathS[65 * j + 64:65 * j + 65, 0:NH]), 16)
            Ss["gtl"].inc(g.dma_start(self.q_as9[8:9, :], self.onesf[0:1, 0:NH]), 16)

        b_g2 = Ss["gtl"].n

        @blk.tensor
        def _(t):
            t.wait_ge(Ss["gtl"].h, b_g2)
            mm = t.matmul(pb[5][0:8, 0:32], self.c_sel9[:], self.q_as9[:], start=True, stop=True)
            Ss["pet"].inc(mm, 1)

        b_p5 = Ss["pet"].n
        b_sue = Ss["sue"].n

        @blk.vector
        def _(v):
            v.wait_ge(Ss["pet"].h, b_p5)
            Ss["sue"].inc(v.tensor_copy(self.a_small[0:8, 0:32], pb[5][0:8, 0:32]), 1)

        @blk.gpsimd
        def _(g):
            g.wait_ge(Ss["sue"].h, b_sue + 1)
            Ss["gtl"].inc(g.dma_start(self.decd[0, :].rearrange("(a b) -> a b", a=8), self.a_small[0:8, 0:32]), 16)

        b_g3 = Ss["gtl"].n

        @blk.sync
        def _(sync):
            sync.wait_ge(Ss["gtl"].h, b_g3)
            Ss["arw"].inc(sync.dma_start(self.a_small[0:1, 64:320], self.decd[:]), 16)

        b_a2 = Ss["arw"].n

        @blk.tensor
        def _(t):
            t.wait_ge(Ss["arw"].h, b_a2)
            mm = t.matmul(pb[5][0:64, 64:320], self.c_onesf[0:1, 0:64], self.a_small[0:1, 64:320],
                          start=True, stop=True)
            Ss["pet"].inc(mm, 1)

        b_p6 = Ss["pet"].n

        @blk.scalar
        def _(s):
            s.wait_ge(Ss["pet"].h, b_p6)
            Ss["aex"].inc(s.activation(self.q_dec[:], pb[5][0:64, 64:320], AF.Exp), 1)

        b_ax = Ss["aex"].n
        b_su3 = Ss["sue"].n

        @blk.sync
        def _(sync):
            for j in range(NCORE):
                if j >= 1:
                    sync.wait_ge(Ss["sin"].h, j)
                Ss["p2a"].inc(sync.dma_start(self.q_su2[:], self.gathS[65 * j:65 * j + 64, :]), 16)

        @blk.vector
        def _(v):
            v.wait_ge(Ss["aex"].h, b_ax)
            for j in range(NCORE):
                v.wait_ge(Ss["p2a"].h, 16 * (j + 1))
                op = None
                for hh in range(NH):
                    sl = slice(128 * hh, 128 * (hh + 1))
                    sc_ = self.q_dec[:, 32 * j + hh:32 * j + hh + 1]
                    if j == 0:
                        op = v.tensor_scalar(self.s_loc[:, sl], self.q_su2[:, sl], sc_, None, ALU.mult)
                    else:
                        op = v.scalar_tensor_tensor(self.s_loc[:, sl], self.q_su2[:, sl], sc_,
                                                    self.s_loc[:, sl], ALU.mult, ALU.add)
                Ss["sin"].inc(op, 1)

    def p2_chunk(self, blk, cc):
        nc = self.nc
        Ss = self.S
        pb = self.pb
        b_p2a = Ss["p2a"].n
        b_p2b = Ss["p2b"].n
        b_p2c = Ss["p2c"].n
        b_p2d = Ss["p2d"].n
        b_p2e = Ss["p2e"].n
        b_p2f = Ss["p2f"].n
        b_aex = Ss["aex"].n
        b_opw = Ss["opw"].n
        b_sin = Ss["sin"].n
        b_scv = Ss["scv"].n

        @blk.sync
        def _(sync):
            if cc == 0:
                sync.wait_ge(Ss["sin"].h, b_sin)
            sync.wait_ge(Ss["p2b"].h, b_p2b)
            sync.wait_ge(Ss["p2e"].h, b_p2e)
            for tt in range(2):
                c0 = 256 * cc + 128 * tt
                Ss["p2a"].inc(sync.dma_start_transpose(
                    self.q_xtm[:, 4096 * tt:4096 * (tt + 1)], self.x_fm[:, c0:c0 + 128]), 16)
            Ss["p2a"].inc(sync.dma_start(self.q_su2[:], self.su_dram[64 * cc:64 * (cc + 1), :]), 16)
            Ss["p2a"].inc(sync.dma_start(self.q_arow, self.aflat[cc, :][None, :]), 16)
            Ss["p2a"].inc(sync.dma_start_transpose(
                self.q_zyn[:, 0:4096], self.z_fm[:, 256 * cc:256 * cc + 128]), 16)

        ld0 = b_p2a + 16 * 5

        @blk.vector
        def _(v):
            v.wait_ge(Ss["scv"].h, b_scv)
            v.wait_ge(Ss["p2c"].h, b_p2c)
            v.tensor_copy(self.q_ccp, self.k_bc[64:128, 256 * cc:256 * (cc + 1)])
            Ss["p2b"].inc(v.tensor_copy(self.s_p16, self.s_loc), 1)
            v.wait_ge(Ss["p2a"].h, b_p2a + 16 * 3)
            op = None
            for tt in range(2):
                for hh in range(NH):
                    sl = slice(4096 * tt + 128 * hh, 4096 * tt + 128 * (hh + 1))
                    op = v.tensor_scalar(self.q_xp[:, sl], self.q_xtm[:, sl],
                                         self.k_dttm[:, 64 * cc + 32 * tt + hh:64 * cc + 32 * tt + hh + 1],
                                         None, ALU.mult)
            Ss["p2b"].inc(op, 1)

        @blk.tensor
        def _(t):
            if cc == 0:
                t.wait_ge(Ss["ipv"].h, Ss["ipv"].n)
                t.wait_ge(Ss["sue"].h, Ss["sue"].n)
            t.wait_ge(Ss["sil"].h, self.conv_sil_done[cc // 4])
            t.wait_ge(Ss["p2b"].h, b_p2b + 1)
            t.wait_ge(Ss["p2d"].h, b_p2d)
            mm = None
            for jt in range(2):
                mm = t.matmul(pb[2][:, 256 * jt:256 * (jt + 1)],
                              self.k_bc[0:64, 256 * cc + 128 * jt:256 * cc + 128 * (jt + 1)],
                              self.q_ccp[:, :],
                              start=True, stop=True)
            Ss["p2c"].inc(mm, 1)

        @blk.vector
        def _(v):
            v.wait_ge(Ss["p2c"].h, b_p2c + 1)
            Ss["p2d"].inc(v.tensor_copy(self.q_sct[:], pb[2][:, 0:512]), 1)

        pd_pos = {}
        pos = b_p2d + 1
        for r in range(33):
            if r >= 1:
                pos += 1
                pd_pos[("mt", r - 1)] = pos
                if (r - 1) % 4 == 3:
                    g = (r - 1) // 4
                    for it in range(2):
                        pos += 1
                        pd_pos[("ev", 2 * g + it)] = pos

        def pc_R(r):
            g, hl = divmod(r, 4)
            return b_p2c + 1 + 6 * g + hl + 1

        def pc_Y(y):
            g, it = divmod(y, 2)
            return b_p2c + 1 + 6 * g + 4 + it + 1

        @blk.tensor
        def _(t):
            t.wait_ge(Ss["p2a"].h, b_p2a + 16 * 4)
            for g in range(8):
                for hl in range(4):
                    r = 4 * g + hl
                    hh = 4 * g + hl
                    bank = pb[hl % 2]
                    if r >= 2:
                        t.wait_ge(Ss["p2f"].h, b_p2f + r - 1)
                    mm = t.matmul(bank[:, 0:256], self.c_onesf[0:1, :],
                                  self.q_arow[0:1, 256 * hh:256 * (hh + 1)], start=True, stop=True)
                    Ss["p2c"].inc(mm, 1)
                t.wait_ge(Ss["p2d"].h, pd_pos[("mt", 4 * g + 3)])
                if g == 0:
                    t.wait_ge(Ss["p2b"].h, b_p2b + 2)
                for it in range(2):
                    y = 2 * g + it
                    ybank = pb[3 + y % 2]
                    if y >= 2:
                        t.wait_ge(Ss["p2d"].h, pd_pos[("ev", y - 2)])
                    mm = t.matmul(ybank[:, 0:512],
                                  self.q_ccp[:, 128 * it:128 * (it + 1)],
                                  self.s_p16[:, 512 * g:512 * (g + 1)],
                                  start=True, stop=False)
                    last = None
                    for hl in range(4):
                        hh = 4 * g + hl
                        for jt in range(it + 1):
                            sp = (hl == 3 and jt == it)
                            last = t.matmul(ybank[:, 128 * hl:128 * (hl + 1)],
                                            self.q_mt[:, 512 * hl + 256 * jt + 128 * it:512 * hl + 256 * jt + 128 * (it + 1)],
                                            self.q_xp[:, 4096 * jt + 128 * hh:4096 * jt + 128 * (hh + 1)],
                                            start=False, stop=sp)
                    Ss["p2c"].inc(last, 1)

        @blk.scalar
        def _(s):
            for r in range(32):
                s.wait_ge(Ss["p2f"].h, b_p2f + r + 1)
                if r >= 2:
                    s.wait_ge(Ss["p2d"].h, pd_pos[("mt", r - 2)])
                par = r % 2
                op = None
                for jt in range(2):
                    op = s.activation(self.q_es[:, 512 * par + 256 * jt:512 * par + 256 * (jt + 1)],
                                      self.q_seg[:, 512 * par + 256 * jt:512 * par + 256 * (jt + 1)],
                                      AF.Exp)
                Ss["aex"].inc(op, 1)

        @blk.vector
        def _(v):
            v.wait_ge(Ss["p2c"].h, b_p2c + 1)
            Ss["p2d"].inc(v.tensor_copy(self.q_sct[:], pb[2][:, 0:512]), 1)
            for r in range(33):
                if r < 32:
                    g, hl = divmod(r, 4)
                    hh = r
                    bank = pb[hl % 2]
                    par = r % 2
                    v.wait_ge(Ss["p2c"].h, pc_R(r))
                    if r >= 2:
                        v.wait_ge(Ss["aex"].h, b_aex + r - 1)
                    op = None
                    for jt in range(2):
                        acol = self.k_atm[:, 64 * cc + 32 * jt + hh:64 * cc + 32 * jt + hh + 1]
                        op = v.scalar_tensor_tensor(
                            self.q_seg[:, 512 * par + 256 * jt:512 * par + 256 * (jt + 1)],
                            bank[:, 0:256], acol,
                            self.c_mask[:, 256 * jt:256 * (jt + 1)],
                            ALU.subtract, ALU.add)
                    Ss["p2f"].inc(op, 1)
                if r >= 1:
                    rr = r - 1
                    par = rr % 2
                    hl = rr % 4
                    v.wait_ge(Ss["aex"].h, b_aex + rr + 1)
                    op = None
                    for jt in range(2):
                        op = v.tensor_tensor(
                            self.q_mt[:, 512 * hl + 256 * jt:512 * hl + 256 * (jt + 1)],
                            self.q_es[:, 512 * par + 256 * jt:512 * par + 256 * (jt + 1)],
                            self.q_sct[:, 256 * jt:256 * (jt + 1)], ALU.mult)
                    Ss["p2d"].inc(op, 1)
                    if rr % 4 == 3:
                        g = rr // 4
                        for it in range(2):
                            y = 2 * g + it
                            ybank = pb[3 + y % 2]
                            v.wait_ge(Ss["p2c"].h, pc_Y(y))
                            op = None
                            for hl2 in range(4):
                                hh2 = 4 * g + hl2
                                ecol = self.k_etm[:, 64 * cc + 32 * it + hh2:64 * cc + 32 * it + hh2 + 1]
                                op = v.tensor_scalar(
                                    self.q_y0[:, 4096 * it + 512 * g + 128 * hl2:4096 * it + 512 * g + 128 * (hl2 + 1)],
                                    ybank[:, 128 * hl2:128 * (hl2 + 1)], ecol, None, ALU.mult)
                            Ss["p2d"].inc(op, 1)

        b_ax2 = b_aex + 32
        rsum = self.a_small[0:128, 320:352]
        rstd = self.a_small[0:128, 352:384]
        rs2 = self.a_small[0:128, 384:416]

        @blk.sync
        def _(sync):
            sync.wait_ge(Ss["p2e"].h, b_p2e + 1)
            Ss["p2a"].inc(sync.dma_start_transpose(
                self.q_zyn[:, 0:4096], self.z_fm[:, 256 * cc + 128:256 * cc + 256]), 16)

        @blk.scalar
        def _(s):
            for it in range(2):
                s.wait_ge(Ss["p2a"].h, (ld0 if it == 0 else b_p2a + 16 * 6))
                if it == 1:
                    s.wait_ge(Ss["p2f"].h, b_p2f + 35)
                op = s.activation(self.q_sig[:], self.q_zyn[:, 0:4096], AF.Sigmoid)
                Ss["p2e"].inc(op, 1)
                s.wait_ge(Ss["p2f"].h, b_p2f + 34 + 3 * it)
                Ss["aex"].inc(s.activation(self.q_sig[:], self.q_y0[:, 4096 * it:4096 * (it + 1)], AF.Square), 1)
                s.wait_ge(Ss["p2f"].h, b_p2f + 35 + 3 * it)
                s.activation(rs2, rsum, AF.Ln, scale=1.0 / 128.0, bias=self.c_cst[:, 1:2])
                Ss["aex"].inc(s.activation(rstd, rs2, AF.Exp, scale=-0.5), 1)

        @blk.vector
        def _(v):
            for it in range(2):
                v.wait_ge(Ss["p2d"].h, pd_pos[("ev", 14 + it)])
                op = None
                for hh in range(NH):
                    sl = slice(4096 * it + 128 * hh, 4096 * it + 128 * (hh + 1))
                    op = v.scalar_tensor_tensor(self.q_y0[:, sl], self.q_xtm[:, sl],
                                                self.c_drep[:, hh:hh + 1], self.q_y0[:, sl],
                                                ALU.mult, ALU.add)
                Ss["p2f"].inc(op, 1)
                v.wait_ge(Ss["p2e"].h, b_p2e + 1 + 2 * it)
                y0it = self.q_y0[:, 4096 * it:4096 * (it + 1)]
                Ss["p2f"].inc(v.tensor_tensor(y0it, y0it, self.q_sig[:], ALU.mult), 1)
                v.wait_ge(Ss["aex"].h, b_ax2 + 1 + 2 * it)
                Ss["p2f"].inc(v.reduce_sum(
                    rsum, self.q_sig[:, 0:4096].rearrange("p (h q) -> p h q", h=NH), axis=AX.X), 1)
                v.wait_ge(Ss["aex"].h, b_ax2 + 2 + 2 * it)
                v.wait_ge(Ss["opw"].h, b_opw + 16 * it)
                op = None
                for hh in range(NH):
                    sl = slice(4096 * it + 128 * hh, 4096 * it + 128 * (hh + 1))
                    op = v.tensor_scalar(self.q_zyn[:, 4096 + 128 * hh:4096 + 128 * (hh + 1)],
                                         self.q_y0[:, sl], rstd[:, hh:hh + 1], None, ALU.mult)
                Ss["p2e"].inc(op, 1)

        @blk.gpsimd
        def _(g):
            for it in range(2):
                g.wait_ge(Ss["p2e"].h, b_p2e + 2 + 2 * it)
                r0 = 256 * cc + 128 * it
                Ss["opw"].inc(g.dma_start(self.yn_tm[r0:r0 + 128, :], self.q_zyn[:, 4096:8192]), 16)

        @blk.vector
        def _(v):
            v.wait_ge(Ss["p2a"].h, b_p2a + 16 * 3)
            v.wait_ge(Ss["p2c"].h, pc_Y(15))
            op = None
            for hh in range(NH):
                sl = slice(128 * hh, 128 * (hh + 1))
                el = self.k_elr[:, 32 * cc + hh:32 * cc + hh + 1]
                v.tensor_scalar(self.s_loc[:, sl], self.s_loc[:, sl], el, None, ALU.mult)
                op = v.scalar_tensor_tensor(self.s_loc[:, sl], self.q_su2[:, sl], el,
                                            self.s_loc[:, sl], ALU.mult, ALU.add)
            Ss["scv"].inc(op, 1)
            Ss["p2b"].inc(v.nop(), 1)

    def p0_ow(self, blk):
        Ss = self.S
        b_uld = Ss["uld"].n
        b_dtv = Ss["dtv"].n
        b_oww = Ss["opv"].n

        @blk.sync
        def _(sync):
            sync.wait_ge(Ss["cvv"].h, self.ln_cvv)
            for i in range(32):
                if i >= 1:
                    sync.wait_ge(Ss["dtv"].h, b_dtv + i)
                Ss["uld"].inc(sync.dma_start(self.a_ubuf[:], self.oww[128 * i:128 * (i + 1), :]), 16)

        @blk.vector
        def _(v):
            v.wait_ge(Ss["xnw"].h, 17 * 16 + 1)
            for i in range(32):
                v.wait_ge(Ss["uld"].h, b_uld + 16 * (i + 1))
                if i >= 1:
                    v.wait_ge(Ss["opv"].h, b_oww + 16 * i)
                op = v.tensor_scalar(self.a_xn0[:, 0:2048], self.a_ubuf[:],
                                     self.c_normw[:, i:i + 1], None, ALU.mult)
                Ss["dtv"].inc(op, 1)

        @blk.gpsimd
        def _(g):
            for i in range(32):
                g.wait_ge(Ss["dtv"].h, b_dtv + i + 1)
                Ss["opv"].inc(g.dma_start(self.ow16[128 * i:128 * (i + 1), :], self.a_xn0[:, 0:2048]), 16)

    def dq_init(self):
        self.dq_units = [(m, i, ch) for m in range(3) for i in range(8) for ch in range(4)]
        self.dq_done = 0
        self.dq_ffa_base = self.S["ffa"].n

    def dequant_piece(self, blk, n_units):
        Ss = self.S
        lo = self.dq_done
        hi = min(lo + n_units, len(self.dq_units))
        self.dq_done = hi
        if lo >= hi:
            return
        pks = [self.pkg, self.pku, self.pkd]
        scs = [self.scg, self.scu, self.scd]


        @blk.vector
        def _(v):
            for idx in range(lo, hi):
                m, i, ch = self.dq_units[idx]
                v.wait_ge(Ss["ffa"].h, self.dq_ffa_target(idx))
                if idx >= 1:
                    v.wait_ge(Ss["dqw"].h, 32 * idx)
                pk16 = self.dq_pk16[idx % 2]
                v.tensor_scalar(self.dq_hi16[:], pk16[:], 4, None, ALU.logical_shift_right)
                v.tensor_scalar(self.dq_hi16[:], self.dq_hi16[:], 0x0F0F, None, ALU.bitwise_and)
                v.tensor_scalar(self.dq_lo16[:], pk16[:], 0x0F0F, None, ALU.bitwise_and)
                op = None
                for half, src8 in ((0, self.dq_lo), (1, self.dq_hi)):
                    c, ge, r, t, sgn = self.dq_c, self.dq_ge, self.dq_r, self.dq_t, self.dq_s
                    v.tensor_copy(c[:], src8[:])
                    v.tensor_scalar(ge[:], c[:], 8.0, None, ALU.is_ge)
                    v.scalar_tensor_tensor(r[:], ge[:], -8.0, c[:], ALU.mult, ALU.add)
                    v.tensor_scalar(sgn[:], ge[:], -2.0, 1.0, ALU.mult, ALU.add)
                    v.tensor_scalar(t[:], r[:], 2.0, -8.0, ALU.mult, ALU.add)
                    v.scalar_tensor_tensor(t[:], r[:], -2.0, t[:], ALU.add, ALU.max)
                    v.scalar_tensor_tensor(t[:], r[:], 0.5, t[:], ALU.mult, ALU.max)
                    v.tensor_tensor(t[:], t[:], sgn[:], ALU.mult)
                    op = v.tensor_tensor(self.dq_o[half][:], t[:],
                                         self.dq_scx[:, 512 * (ch % 2):512 * (ch % 2 + 1)], ALU.mult)
                Ss["dqv"].inc(op, 1)

        @blk.gpsimd
        def _(g):
            def outs(idx):
                m, i, ch = self.dq_units[idx]
                g.wait_ge(Ss["dqv"].h, idx + 1)
                r0 = 2048 * m + 256 * i
                cl = slice(512 * ch, 512 * (ch + 1))
                Ss["dqw"].inc(g.dma_start(self.cinW[r0:r0 + 256:2, cl], self.dq_o[0][:]), 16)
                Ss["dqw"].inc(g.dma_start(self.cinW[r0 + 1:r0 + 256:2, cl], self.dq_o[1][:]), 16)

            for idx in range(lo, hi):
                m, i, ch = self.dq_units[idx]
                if idx > lo:
                    outs(idx - 1)
                if idx >= 2:
                    g.wait_ge(Ss["dqv"].h, idx - 1)
                Ss["ffa"].inc(g.dma_start(
                    self.dq_pk[idx % 2][:, :],
                    pks[m][128 * i:128 * (i + 1), 512 * ch:512 * (ch + 1)]), 16)
                if ch % 2 == 0:
                    if idx >= 1:
                        g.wait_ge(Ss["dqv"].h, idx)
                    c0 = 1024 * (ch // 2)
                    for grp in range(8):
                        Ss["ffa"].inc(g.dma_start(
                            self.dq_scx[16 * grp:16 * (grp + 1), :],
                            scs[m][8 * i + grp, None, c0:c0 + 1024].broadcast_to((16, 1024))), 16)
            outs(hi - 1)

        if hi == len(self.dq_units):
            b_rs = Ss["m2"].n

            @blk.scalar
            def _(s_):
                s_.wait_ge(Ss["dqw"].h, 32 * len(self.dq_units))
                for m in range(2):
                    for j in range(16):
                        srcv = self.cinW[2048 * m:2048 * (m + 1), 128 * j:128 * (j + 1)]
                        srcv = srcv.rearrange("(kt p) nl -> p kt nl", p=128)
                        dstv = self.cinW2[2048 * m + 128 * j:2048 * m + 128 * (j + 1), :]
                        dstv = dstv.rearrange("p (kt nl) -> p kt nl", kt=16)
                        Ss["m2"].inc(s_.dma_start(dstv, srcv), 16)
                Ss["m2"].inc(s_.dma_start(self.cinW2[4096:6144, :], self.cinW[4096:6144, :]), 16)

            @blk.gpsimd
            def _(g):
                g.wait_ge(Ss["m2"].h, b_rs + 33 * 16)
                g.collective_compute(
                    "AllGather", mybir.AluOpType.bypass,
                    replica_groups=[list(range(NCORE))],
                    ins=[self.cinW2.ap().opt()],
                    outs=[self.gathW.ap().opt()],
                ).then_inc(Ss["ccw"].h, 1)
                Ss["ccw"].n += 1

    def dq_ffa_target(self, idx):
        tot = self.dq_ffa_base
        for k in range(idx + 1):
            m, i, ch = self.dq_units[k]
            tot += 16 + (128 if ch % 2 == 0 else 0)
        return tot

    def layout_p25(self):
        self.o_ynt = [self.arena[:, 2048 * i:2048 * (i + 1)].bitcast(BF16) for i in range(32)]
        self.o_ow = [self.arena[:, 65536 + 32768 * p:65536 + 32768 * (p + 1)].bitcast(BF16)
                     for p in range(2)]
        self.o_ev = [self.arena[:, 131072 + 2048 * p:131072 + 2048 * (p + 1)].bitcast(F32)
                     for p in range(2)]

    def p25_half(self, blk, H):
        Ss = self.S
        pb = self.pb
        b_ynl = Ss["gtl"].n
        b_owl = Ss["ffw"].n
        b_ope = Ss["ope"].n
        b_opv = Ss["p2f"].n
        b_opwE = Ss["ffc"].n
        b_opwO = Ss["ffd"].n
        r0 = 1024 * H

        @blk.sync
        def _(sync):
            sync.wait_ge(Ss["opw"].h, 128 * (H + 1))
            sync.wait_ge(Ss["ope"].h, b_ope)
            if H == 0:
                sync.wait_ge(Ss["p2e"].h, Ss["p2e"].n)
            for hp in range(32):
                Ss["gtl"].inc(sync.dma_start_transpose(
                    self.o_ynt[hp][:, 0:1024], self.yn_tm[r0:r0 + 1024, 128 * hp:128 * (hp + 1)]), 16)
            sync.wait_ge(Ss["opv"].h, Ss["opv"].n)
            for eg in range(4):
                if eg >= 2:
                    sync.wait_ge(Ss["ope"].h, b_ope + 16)
                Ss["ffw"].inc(sync.dma_start(
                    self.o_ow[eg % 2][:].rearrange("p (t n) -> p t n", t=32),
                    self.ow16[:, 512 * eg:512 * (eg + 1)].rearrange("(t p) n -> p t n", p=128)), 16)

        @blk.tensor
        def _(t):
            t.wait_ge(Ss["opv"].h, Ss["opv"].n)
            t.wait_ge(Ss["p2d"].h, self.p2d_total)
            if H == 1:
                t.wait_ge(Ss["p2f"].h, b_opv)
            t.wait_ge(Ss["gtl"].h, b_ynl + 16 * 32)
            for egp in range(2):
                t.wait_ge(Ss["ffw"].h, b_owl + 16 * (2 * egp + 2))
                for tt in range(8):
                    bks = [pb[2 * (tt % 2)], pb[2 * (tt % 2) + 1]]
                    if tt >= 2:
                        t.wait_ge(Ss["p2f"].h, b_opv + 16 * egp + 2 * (tt - 2) + 2)
                    elif egp == 1:
                        t.wait_ge(Ss["p2f"].h, b_opv + 2 * (tt + 6) + 2)
                    for hp in range(32):
                        wy = self.o_ynt[hp][:, 128 * tt:128 * (tt + 1)]
                        for e2 in range(2):
                            mm = t.matmul(bks[e2][:, 0:512], wy,
                                          self.o_ow[e2][:, 512 * hp:512 * (hp + 1)],
                                          start=(hp == 0), stop=(hp == 31))
                    Ss["ope"].inc(mm, 2)

        @blk.vector
        def _(v):
            for egp in range(2):
                for tt in range(8):
                    v.wait_ge(Ss["ope"].h, b_ope + 16 * egp + 2 * (tt + 1))
                    for e2 in range(2):
                        y = 16 * egp + 2 * tt + e2
                        bank = pb[2 * (tt % 2) + e2]
                        if not (H == 0 and y < 2):
                            sm = Ss["ffc"] if y % 2 == 0 else Ss["ffd"]
                            bs = b_opwE if y % 2 == 0 else b_opwO
                            v.wait_ge(sm.h, bs + 16 * (y // 2))
                        op = v.tensor_copy(self.o_ev[y % 2][:], bank[:, 0:512])
                        Ss["p2f"].inc(op, 1)

        @blk.gpsimd
        def _(g):
            for egp in range(2):
                for tt in range(8):
                    for e2 in range(2):
                        y = 16 * egp + 2 * tt + e2
                        eg = 2 * egp + e2
                        g.wait_ge(Ss["p2f"].h, b_opv + y + 1)
                        rr = r0 + 128 * tt
                        Ss["ffc" if y % 2 == 0 else "ffd"].inc(g.dma_start(
                            self.out[rr:rr + 128, 512 * eg:512 * (eg + 1)], self.o_ev[y % 2][:]), 16)

    def layout_p3(self):
        a = self.arena
        self.f_xnt = [a[:, 2048 * k:2048 * (k + 1)].bitcast(BF16) for k in range(16)]
        self.f_int = a[:, 32768:65536].bitcast(BF16)
        self.f_wg = [a[:, 65536 + 4096 * p:65536 + 4096 * (p + 1)].bitcast(BF16) for p in range(2)]
        self.f_wd = [a[:, 73728 + 16384 * p:73728 + 16384 * (p + 1)].bitcast(BF16) for p in range(2)]
        self.f_gs = a[:, 106496:108544].bitcast(F32)
        self.f_us = a[:, 108544:110592].bitcast(F32)
        self.f_s2 = a[:, 110592:112640].bitcast(F32)
        self.f_sg = a[:, 112640:114688].bitcast(F32)
        self.f_ev = [a[:, 114688 + 2048 * p:114688 + 2048 * (p + 1)].bitcast(F32) for p in range(2)]
        self.f_ut = a[:, 118784:126976].bitcast(F32)

    def p3(self, blk):
        Ss = self.S
        ffc_oproj = (Ss["ffc"].n, Ss["ffd"].n)
        for TH in range(2):
            self.p3_half(blk, TH, ffc_oproj)


    def p3_half(self, blk, TH, ffc_oproj):
        Ss = self.S
        pb = self.pb
        r0 = 1024 * TH
        b_xl = Ss["gtl"].n
        b_ui = Ss["uld"].n
        b_uacc = Ss["dtv"].n
        ffa_prev = Ss["ffa"].n

        @blk.sync
        def _(sync):
            sync.wait_ge(Ss["ccw"].h, 1)
            sync.wait_ge(Ss["ffc"].h, ffc_oproj[0])
            sync.wait_ge(Ss["ffd"].h, ffc_oproj[1])
            if TH == 1:
                sync.wait_ge(Ss["ffa"].h, ffa_prev)
            for k in range(16):
                Ss["gtl"].inc(sync.dma_start_transpose(
                    self.f_xnt[k][:, 0:1024],
                    self.xn_tm[3 + r0:3 + r0 + 1024, 128 * k:128 * (k + 1)]), 16)
            for i in range(8):
                if i >= 1:
                    sync.wait_ge(Ss["dtv"].h, b_uacc + 16 * i)
                rr = r0 + 128 * i
                Ss["uld"].inc(sync.dma_start(self.f_ut[:, 0:2048], self.u[rr:rr + 128, :]), 16)

        @blk.gpsimd
        def _(g):
            g.wait_ge(Ss["ffc"].h, ffc_oproj[0])
            g.wait_ge(Ss["ffd"].h, ffc_oproj[1])
            for i in range(8):
                g.wait_ge(Ss["uld"].h, b_ui + 16 * (i + 1))
                rr = r0 + 128 * i
                Ss["dtv"].inc(g.dma_start(self.out[rr:rr + 128, :], self.f_ut[:, 0:2048],
                                          accum_op=ALU.add), 16)

        for s in range(8):
            self.p3_slab(blk, TH, s, b_xl + 16 * 16, b_uacc + 16 * 8)

    def p3_slab(self, blk, TH, s, xnt_ready, uacc_done):
        Ss = self.S
        pb = self.pb
        r0 = 1024 * TH
        b_wlE = Ss["ffb"].n
        b_wlO = Ss["ffg"].n
        b_wd = Ss["ffw"].n
        b_ge = Ss["ffa"].n
        b_gv = Ss["p2d"].n
        b_pa = Ss["p2a"].n
        b_ax = Ss["aex"].n
        b_dn = Ss["ope"].n
        b_dv = Ss["p2f"].n
        b_dwE = Ss["ffc"].n
        b_dwO = Ss["ffd"].n
        grow = 6144 * s
        urow = 6144 * s + 2048
        drow = 6144 * s + 4096

        @blk.sync
        def _(sync):
            sync.wait_ge(Ss["gtl"].h, xnt_ready)
            for nt in range(16):
                for gu, row in ((0, grow), (1, urow)):
                    it = 2 * nt + gu
                    if it >= 2:
                        sync.wait_ge(Ss["ffa"].h, b_ge + 2 * (it - 1))
                    else:
                        sync.wait_ge(Ss["ffa"].h, max(0, b_ge - 2 * (1 - it)))
                    Ss["ffb" if it % 2 == 0 else "ffg"].inc(sync.dma_start(
                        self.f_wg[it % 2][:],
                        self.gathW[row + 128 * nt:row + 128 * (nt + 1), :]), 16)
            for eg in range(4):
                if eg >= 2:
                    sync.wait_ge(Ss["ope"].h, b_dn + 16)
                else:
                    sync.wait_ge(Ss["ope"].h, b_dn)
                Ss["ffw"].inc(sync.dma_start(
                    self.f_wd[eg % 2][:].rearrange("p (k n) -> p k n", k=16),
                    self.gathW[drow:drow + 2048, 512 * eg:512 * (eg + 1)].rearrange(
                        "(k p) n -> p k n", p=128)), 16)

        @blk.tensor
        def _(t):
            if TH == 0 and s == 0:
                t.wait_ge(Ss["p2f"].h, self.oproj_evac_total)
            for nt in range(16):
                for gu in range(2):
                    it = 2 * nt + gu
                    t.wait_ge(Ss["ffb" if it % 2 == 0 else "ffg"].h,
                              (b_wlE if it % 2 == 0 else b_wlO) + 16 * (it // 2 + 1))
                    if nt == 0 and not (TH == 0 and s == 0):
                        t.wait_ge(Ss["p2d"].h, b_gv - 1)
                    if nt >= 1:
                        for tg in range(2):
                            m_prev = 2 * (nt - 1) + tg
                            t.wait_ge(Ss["p2d"].h, b_gv + 3 * m_prev + 1 + gu)
                    banks = [pb[(4 * nt + 2 * gu + tg) % 4] for tg in range(2)]
                    for k in range(16):
                        wk = self.f_wg[it % 2][:, 128 * k:128 * (k + 1)]
                        for tg in range(2):
                            mm = t.matmul(banks[tg][:, 0:512], wk,
                                          self.f_xnt[k][:, 512 * tg:512 * (tg + 1)],
                                          start=(k == 0), stop=(k == 15))
                            if k == 15:
                                Ss["ffa"].inc(mm, 1)
            for egp in range(2):
                t.wait_ge(Ss["ffw"].h, b_wd + 16 * (2 * egp + 2))
                t.wait_ge(Ss["p2d"].h, b_gv + 96)
                for tt in range(8):
                    bks = [pb[4 + 2 * (tt % 2)], pb[5 + 2 * (tt % 2)]]
                    if tt >= 2:
                        t.wait_ge(Ss["p2f"].h, b_dv + 16 * egp + 2 * (tt - 2) + 2)
                    elif egp == 1:
                        t.wait_ge(Ss["p2f"].h, b_dv + 2 * (tt + 6) + 2)
                    for nl in range(16):
                        wi = self.f_int[:, 1024 * nl + 128 * tt:1024 * nl + 128 * (tt + 1)]
                        for e2 in range(2):
                            mm = t.matmul(bks[e2][:, 0:512], wi,
                                          self.f_wd[e2][:, 512 * nl:512 * (nl + 1)],
                                          start=(nl == 0), stop=(nl == 15))
                    Ss["ope"].inc(mm, 2)

        @blk.vector
        def _(v):
            v.wait_ge(Ss["ope"].h, b_dn)
            for nt in range(16):
                for tg in range(2):
                    m = 2 * nt + tg
                    qg = 4 * nt + tg
                    qu = 4 * nt + 2 + tg
                    v.wait_ge(Ss["ffa"].h, b_ge + qg + 1)
                    Ss["p2d"].inc(v.tensor_copy(self.f_gs[:], pb[qg % 4][:, 0:512]), 1)
                    v.wait_ge(Ss["ffa"].h, b_ge + qu + 1)
                    Ss["p2d"].inc(v.tensor_copy(self.f_us[:], pb[qu % 4][:, 0:512]), 1)
                    v.tensor_tensor(self.f_s2[:], self.f_gs[:], self.f_gs[:], ALU.mult)
                    v.tensor_scalar(self.f_s2[:], self.f_s2[:], 0.044715, 1.0, ALU.mult, ALU.add)
                    Ss["p2a"].inc(v.tensor_tensor(self.f_s2[:], self.f_s2[:], self.f_gs[:], ALU.mult), 1)
                    v.wait_ge(Ss["aex"].h, b_ax + m + 1)
                    v.tensor_tensor(self.f_gs[:], self.f_gs[:], self.f_sg[:], ALU.mult)
                    ic = v.tensor_tensor(
                        self.f_int[:, 1024 * nt + 512 * tg:1024 * nt + 512 * (tg + 1)],
                        self.f_gs[:], self.f_us[:], ALU.mult)
                    Ss["p2d"].inc(ic, 1)
            for egp in range(2):
                for tt in range(8):
                    v.wait_ge(Ss["ope"].h, b_dn + 16 * egp + 2 * (tt + 1))
                    for e2 in range(2):
                        y = 16 * egp + 2 * tt + e2
                        bank = pb[4 + 2 * (tt % 2) + e2]
                        if not (TH == 0 and s == 0 and y < 2):
                            sm = Ss["ffc"] if y % 2 == 0 else Ss["ffd"]
                            bs = b_dwE if y % 2 == 0 else b_dwO
                            v.wait_ge(sm.h, bs + 16 * (y // 2))
                        Ss["p2f"].inc(v.tensor_copy(self.f_ev[y % 2][:], bank[:, 0:512]), 1)

        @blk.scalar
        def _(s_):
            for nt in range(16):
                for tg in range(2):
                    m = 2 * nt + tg
                    s_.wait_ge(Ss["p2a"].h, b_pa + m + 1)
                    if m >= 1:
                        s_.wait_ge(Ss["p2d"].h, b_gv + 3 * (m - 1) + 3)
                    Ss["aex"].inc(s_.activation(self.f_sg[:], self.f_s2[:], AF.Sigmoid,
                                                scale=1.5957691216057308), 1)

        @blk.gpsimd
        def _(g):
            if s == 0:
                g.wait_ge(Ss["dtv"].h, uacc_done)
            for egp in range(2):
                for tt in range(8):
                    for e2 in range(2):
                        y = 16 * egp + 2 * tt + e2
                        eg = 2 * egp + e2
                        g.wait_ge(Ss["p2f"].h, b_dv + y + 1)
                        if y == 0 and not (TH == 0 and s == 0):
                            g.wait_ge(Ss["ffc"].h, b_dwE)
                            g.wait_ge(Ss["ffd"].h, b_dwO)
                        rr = r0 + 128 * tt
                        Ss["ffc" if y % 2 == 0 else "ffd"].inc(g.dma_start(
                            self.out[rr:rr + 128, 512 * eg:512 * (eg + 1)], self.f_ev[y % 2][:],
                            accum_op=ALU.add), 16)

    def finish_stage(self, blk, wait_list):
        Ss = self.S

        @blk.gpsimd
        def _(g):
            for sem, val in wait_list:
                g.wait_ge(sem.h, val)
            st = Ss["outw"]
            for r in range(16):
                st.inc(g.dma_start(self.out[128 * r:128 * (r + 1), :], self.a_ubuf[:, 0:2048]), 16)
            for t, src in self.tap_list:
                st.inc(g.dma_start(t[:], src), 16)
            g.wait_ge(st.h, st.n)

    def build_full(self):
        self.declare_io()
        self.alloc_sbuf()
        self.alloc_psum()
        self.make_sems()
        self.layout_p01()
        self.layout_p2()
        self.layout_p25()
        self.layout_p3()
        self.conv_sil_done = {}
        with self.nc.Block() as blk:
            self.p0(blk)
            self.p0_ow(blk)
            self.dq_init()
            self.dequant_piece(blk, 48)
            self.p1_half(blk, 0)
            self.dequant_piece(blk, 48)
            self.p1_half(blk, 1)
            self.p2_pre(blk)
            for cc in range(NCH):
                self.p2_chunk(blk, cc)
            self.p2d_total = self.S["p2d"].n
            self.p25_half(blk, 0)
            self.p25_half(blk, 1)
            self.oproj_evac_total = self.S["p2f"].n
            self.p3(blk)
            self.add_tap_dram("gathW", self.gathW[0:6144, :], [6144, 2048], BF16)
            self.add_tap_dram("attnO", self.attn[:], [T, EMBED], F32)
            self.add_tap_dram("yn", self.yn_tm[:], [T, DIN], BF16)
            Ss = self.S

            @blk.gpsimd
            def _(g):
                g.wait_ge(Ss["ffc"].h, Ss["ffc"].n)
                g.wait_ge(Ss["ffd"].h, Ss["ffd"].n)
                g.wait_ge(Ss["dtv"].h, Ss["dtv"].n)
                st = Ss["outw"]
                for t, srcap in self.tap_list:
                    st.inc(g.dma_start(t[:], srcap), 16)
                g.wait_ge(st.h, st.n)
        return self.nc

    def build_stage3(self):
        self.declare_io()
        self.alloc_sbuf()
        self.alloc_psum()
        self.make_sems()
        self.layout_p01()
        self.layout_p2()
        self.conv_sil_done = {}
        with self.nc.Block() as blk:
            self.p0(blk)
            for h in range(2):
                self.p1_half(blk, h)
            self.p2_pre(blk)
            for cc in range(NCH):
                self.p2_chunk(blk, cc)
            self.add_tap_dram("yn", self.yn_tm[:], [T, DIN], BF16)
            if "prev" in self.taps:
                t = self.nc.dram_tensor("tap_prev", [64, DIN], F32, kind="ExternalOutput")
                self.tap_list.append((t, self.s_loc))
            if "sin" in self.taps:
                pass
            wl = [(self.S[x], self.S[x].n) for x in
                  ["scv", "opw", "p2e", "aex", "p2d", "suw", "zfw", "xfw"]]
            self.finish_stage(blk, wl)
        return self.nc

    def build_stage2(self):
        self.declare_io()
        self.alloc_sbuf()
        self.alloc_psum()
        self.make_sems()
        self.layout_p01()
        self.conv_sil_done = {}
        with self.nc.Block() as blk:
            self.p0(blk)
            for h in range(2):
                self.p1_half(blk, h)
            self.add_tap_dram("xn", self.xn_tm[0:2051, :], [2051, EMBED], BF16)
            self.add_tap_dram("zfm", self.z_fm[:], [DIN, T], BF16)
            self.add_tap_dram("xfm", self.x_fm[:], [DIN, T], BF16)
            self.add_tap_dram("su", self.su_dram[:], [NCH * DSTATE, DIN], F32)
            self.add_tap_dram("aflat", self.aflat[:], [NCH, NH * CHUNK], F32)
            if "sloc" in self.taps:
                t = self.nc.dram_tensor("tap_sloc", [64, DIN], F32, kind="ExternalOutput")
                self.tap_list.append((t, self.s_loc))
            if "dtpos" in self.taps:
                t = self.nc.dram_tensor("tap_dtpos", [32, T], F32, kind="ExternalOutput")
                self.tap_list.append((t, self.k_dtpos))
            if "etm" in self.taps:
                t = self.nc.dram_tensor("tap_etm", [128, NCH * 64], F32, kind="ExternalOutput")
                self.tap_list.append((t, self.k_etm[:]))
            if "atm" in self.taps:
                t = self.nc.dram_tensor("tap_atm", [128, NCH * 64], F32, kind="ExternalOutput")
                self.tap_list.append((t, self.k_atm[:]))
            if "dttm" in self.taps:
                t = self.nc.dram_tensor("tap_dttm", [128, NCH * 64], F32, kind="ExternalOutput")
                self.tap_list.append((t, self.k_dttm[:]))
            wl = [(self.S[x], self.S[x].n) for x in
                  ["scv", "suw", "zfw", "xfw", "aex", "sue"]]
            self.finish_stage(blk, wl)
        return self.nc


def host_prep(inputs, core):
    b, q = core // 4, core % 4
    u_full = np.asarray(inputs["L0_attn_u"])
    u = np.ascontiguousarray(u_full[b, 2048 * q:2048 * (q + 1)], dtype=np.float32)
    if q == 0:
        uhalo = np.zeros((3, EMBED), np.float32)
    else:
        uhalo = np.ascontiguousarray(u_full[b, 2048 * q - 3:2048 * q], dtype=np.float32)

    W = np.asarray(inputs["L0_attn_in_proj_w"]).astype(np.float32)
    Wp = np.zeros((NCT * 128, EMBED), np.float32)
    Wp[0:8352] = W
    tiles = Wp.reshape(NCT, 128, KT, 128).transpose(0, 2, 3, 1)
    ipw = np.ascontiguousarray(tiles.reshape(NCT * KT * 128, 128))

    oww = np.ascontiguousarray(np.asarray(inputs["L0_attn_o_proj_w"]).astype(np.float32).T)

    cw = np.asarray(inputs["L0_attn_conv_w"]).astype(np.float32)
    convw = np.ascontiguousarray(cw.reshape(33, 128, 4).transpose(1, 0, 2).reshape(128, 132))
    cb = np.asarray(inputs["L0_attn_conv_b"]).astype(np.float32)
    convb = np.ascontiguousarray(cb.reshape(33, 128).T)

    alog = np.ascontiguousarray(np.asarray(inputs["L0_attn_A_log"]).astype(np.float32).reshape(NH, 1))
    dtb = np.ascontiguousarray(np.asarray(inputs["L0_attn_dt_bias"]).astype(np.float32).reshape(NH, 1))
    dskip = np.ascontiguousarray(np.asarray(inputs["L0_attn_D_skip"]).astype(np.float32).reshape(1, NH))
    normw = np.ascontiguousarray(np.asarray(inputs["L0_attn_norm_w"]).astype(np.float32))

    constv = np.ascontiguousarray(np.broadcast_to(
        np.array([1e-5, 1e-6, 1.0, 0.5], np.float32), (128, 4)))
    idbf = np.eye(128, dtype=BF)
    idf32 = np.eye(128, dtype=np.float32)
    m = np.zeros((2, 128, 256), np.float32)
    for jt in range(2):
        jl = np.arange(128)[:, None] + jt * 128
        ii = np.arange(256)[None, :]
        m[jt] = np.where(ii >= jl, 0.0, -1e30)
    maskt = np.ascontiguousarray(m.reshape(256, 256))
    ones32 = np.ones((NH, 256), np.float32)
    onesf = np.ones((1, 128), np.float32)

    sel = np.zeros((9, 8), np.float32)
    for j in range(8):
        same = (j // 4 == core // 4) and (j < core)
        if same:
            for mm_ in range(j + 1, core):
                sel[mm_, j] = 1.0
        else:
            sel[8, j] = -1e30
    sel9 = np.ascontiguousarray(sel)

    def packu8(x):
        a = np.asarray(x).astype(np.int32)
        return np.ascontiguousarray(a.view(np.uint8).reshape(a.shape[0], a.shape[1], 4)[:, :, 0])

    pg = packu8(inputs["L0_ffn_gate_W_packed"]).T
    pu = packu8(inputs["L0_ffn_up_W_packed"]).T
    pd = packu8(inputs["L0_ffn_down_W_packed"]).T
    n0 = 2048 * core
    pkg = np.ascontiguousarray(pg[:, n0:n0 + 2048])
    pku = np.ascontiguousarray(pu[:, n0:n0 + 2048])
    pkd = np.ascontiguousarray(pd[1024 * core:1024 * (core + 1), :])
    sg = np.asarray(inputs["L0_ffn_gate_scales"]).astype(np.float32).T
    su_ = np.asarray(inputs["L0_ffn_up_scales"]).astype(np.float32).T
    sd_ = np.asarray(inputs["L0_ffn_down_scales"]).astype(np.float32).T
    scg = np.ascontiguousarray(sg[:, n0:n0 + 2048])
    scu = np.ascontiguousarray(su_[:, n0:n0 + 2048])
    scd = np.ascontiguousarray(sd_[64 * core:64 * (core + 1), :])

    return dict(u=u, uh=uhalo, ipw=ipw, oww=oww, convw=convw, convb=convb,
                alog=alog, dtb=dtb, dskip=dskip, normw=normw, idbf=idbf,
                idf32=idf32, maskt=maskt, ones32=ones32, onesf=onesf, sel9=sel9, constv=constv,
                pkg=pkg, pku=pku, pkd=pkd, scg=scg, scu=scu, scd=scd)


from concourse.bass_utils import run_bass_kernel_spmd


def kernel(**inputs):
    b = Build()
    nc = b.build_full()
    ims = [host_prep(inputs, c) for c in range(NCORE)]
    res = run_bass_kernel_spmd(nc, ims, core_ids=list(range(NCORE)))
    out = np.zeros((2, 8192, EMBED), np.float32)
    for c in range(NCORE):
        out[c // 4, 2048 * (c % 4):2048 * (c % 4 + 1)] = res.results[c]["outp"]
    return out


# version 04 (final)
# speedup vs baseline: 1.1371x; 1.1371x over previous
import numpy as np
import ml_dtypes
import concourse.bass as bass
import concourse.mybir as mybir

F32 = mybir.dt.float32
BF16 = mybir.dt.bfloat16
U8 = mybir.dt.uint8
U16 = mybir.dt.uint16
AF = mybir.ActivationFunctionType
ALU = mybir.AluOpType
AX = mybir.AxisListType
BF = ml_dtypes.bfloat16

EMBED = 2048
DIN = 4096
NH = 32
DSTATE = 64
CHUNK = 256
T = 2048
NCH = 8
CONVD = 4224
MLP = 16384
NCORE = 8
KT = 16
NCT = 66
TPAD = 2176


class S:
    def __init__(self, nc, name):
        self.h = nc.alloc_semaphore(name)
        self.n = 0

    def inc(self, instr, k=1):
        instr.then_inc(self.h, k)
        self.n += k
        return self.n


SEM_NAMES = [
    "ini", "cld", "uld", "wldA", "wslb", "xnw", "xnt", "ipe", "ipv", "zfw",
    "cvv", "sil", "xfw", "dtv", "mil", "afw", "arw", "aex", "xtl", "xpv",
    "pet", "stm", "sue", "suw", "scv", "ccw", "ccs", "gtl", "sin",
    "p2a", "p2b", "p2c", "p2d", "p2e", "p2f", "ope", "opv", "opw", "ffa", "ffb",
    "ffc", "ffd", "ffw", "ffg", "ffh", "dqv", "dqw", "outw", "m2",
]


class Build:
    def __init__(self, taps=()):
        self.taps = set(taps)
        self.nc = bass.Bass("TRN2", target_bir_lowering=False)
        self.tap_list = []

    def declare_io(self):
        nc = self.nc
        self.u = nc.dram_tensor("u", [T, EMBED], F32, kind="ExternalInput")
        self.uh = nc.dram_tensor("uh", [3, EMBED], F32, kind="ExternalInput")
        self.ipw = nc.dram_tensor("ipw", [NCT * KT * 128, 128], F32, kind="ExternalInput")
        self.oww = nc.dram_tensor("oww", [DIN, EMBED], F32, kind="ExternalInput")
        self.convw = nc.dram_tensor("convw", [128, 33 * 4], F32, kind="ExternalInput")
        self.convb = nc.dram_tensor("convb", [128, 33], F32, kind="ExternalInput")
        self.alog = nc.dram_tensor("alog", [NH, 1], F32, kind="ExternalInput")
        self.dtb = nc.dram_tensor("dtb", [NH, 1], F32, kind="ExternalInput")
        self.dskip = nc.dram_tensor("dskip", [1, NH], F32, kind="ExternalInput")
        self.normw = nc.dram_tensor("normw", [DIN], F32, kind="ExternalInput")
        self.idbf = nc.dram_tensor("idbf", [128, 128], BF16, kind="ExternalInput")
        self.idf32 = nc.dram_tensor("idf32", [128, 128], F32, kind="ExternalInput")
        self.maskt = nc.dram_tensor("maskt", [2 * 128, 256], F32, kind="ExternalInput")
        self.ones32 = nc.dram_tensor("ones32", [NH, 256], F32, kind="ExternalInput")
        self.onesf = nc.dram_tensor("onesf", [1, 128], F32, kind="ExternalInput")
        self.sel9 = nc.dram_tensor("sel9", [9, 8], F32, kind="ExternalInput")
        self.constv = nc.dram_tensor("constv", [128, 4], F32, kind="ExternalInput")
        self.pkg = nc.dram_tensor("pkg", [1024, 2048], U8, kind="ExternalInput")
        self.pku = nc.dram_tensor("pku", [1024, 2048], U8, kind="ExternalInput")
        self.pkd = nc.dram_tensor("pkd", [1024, 2048], U8, kind="ExternalInput")
        self.scg = nc.dram_tensor("scg", [64, 2048], F32, kind="ExternalInput")
        self.scu = nc.dram_tensor("scu", [64, 2048], F32, kind="ExternalInput")
        self.scd = nc.dram_tensor("scd", [64, 2048], F32, kind="ExternalInput")
        self.out = nc.dram_tensor("outp", [T, EMBED], F32, kind="ExternalOutput")
        self.xn_tm = nc.dram_tensor("xn_tm", [TPAD, EMBED], BF16, kind="Internal")
        self.ipw16 = nc.dram_tensor("ipw16", [NCT * KT * 128, 128], BF16, kind="Internal")
        self.ow16 = nc.dram_tensor("ow16", [DIN, EMBED], BF16, kind="Internal")
        self.x_fm = nc.dram_tensor("x_fm", [DIN, T], BF16, kind="Internal")
        self.z_fm = nc.dram_tensor("z_fm", [DIN, T], BF16, kind="Internal")
        self.su_dram = nc.dram_tensor("su_dram", [NCH * DSTATE, DIN], F32, kind="Internal")
        self.aflat = nc.dram_tensor("aflat", [NCH, NH * CHUNK], F32, kind="Internal")
        self.asd = nc.dram_tensor("asd", [1, NH], F32, kind="Internal")
        self.decd = nc.dram_tensor("decd", [1, 256], F32, kind="Internal")
        self.yn_tm = nc.dram_tensor("yn_tm", [T, DIN], BF16, kind="Internal")
        self.attn = nc.dram_tensor("attn", [T, EMBED], F32, kind="Internal")
        self.cinS = nc.dram_tensor("cinS", [DSTATE + 1, DIN], F32, kind="Internal")
        self.gathS = nc.dram_tensor("gathS", [NCORE * (DSTATE + 1), DIN], F32, kind="Internal")
        self.cinW = nc.dram_tensor("cinW", [3 * 2048, 2048], BF16, kind="Internal")
        self.gathW = nc.dram_tensor("gathW", [NCORE * 3 * 2048, 2048], BF16, kind="Internal")

    def alloc_sbuf(self):
        nc = self.nc
        A = nc.alloc_sbuf_tensor
        self.c_idb = A("c_idb", [128, 128], BF16)
        self.c_idf = A("c_idf", [128, 128], F32)
        self.c_mask = A("c_mask", [128, 2 * 256], F32)
        self.c_convw = A("c_convw", [128, 33 * 4], F32)
        self.c_convb = A("c_convb", [128, 33], F32)
        self.c_normw = A("c_normw", [128, 32], F32)
        self.c_drep = A("c_drep", [128, NH], F32)
        self.c_ones32 = A("c_ones32", [NH, 256], F32)
        self.c_onesf = A("c_onesf", [1, 128], F32)
        self.c_sel9 = A("c_sel9", [9, 8], F32)
        self.c_acol = A("c_acol", [NH, 1], F32)
        self.c_dtb = A("c_dtb", [NH, 1], F32)
        self.c_init = A("c_init", [128, 16], F32)
        self.c_cst = A("c_cst", [128, 4], F32)
        self.k32 = A("k32", [NH, 8720], F32)
        self.k_a_fm = self.k32[:, 0:2048]
        self.k_dtpos = self.k32[:, 2048:4096]
        self.k_dA = self.k32[:, 4096:5120]
        self.k_d2e = self.k32[:, 5120:5376]
        self.k_de2f = self.k32[:, 5376:5632]
        self.k_t1 = self.k32[:, 5632:6656]
        self.k_t2 = self.k32[:, 6656:7680]
        self.k_dtraw = self.k32[:, 7680:8707]
        self.c_asum = self.k32[:, 8707:8708]
        self.k_atm = A("k_atm", [128, NCH * 64], F32)
        self.k_etm = A("k_etm", [128, NCH * 64], F32)
        self.k_dttm = A("k_dttm", [128, NCH * 64], F32)
        self.k_de2 = A("k_de2", [128, 64], F32)
        self.k_btm = A("k_btm", [128, 128], BF16)
        self.k_bc = A("k_bc", [128, NCH * CHUNK], BF16)
        self.s_pair = A("s_pair", [128, DIN], F32)
        self.s_loc = self.s_pair[0:64, :]
        self.s_su = self.s_pair[64:128, :]
        self.s_b16 = A("s_b16", [128, DIN], BF16)
        self.s_p16 = self.s_b16[0:64, :]
        self.k_elr = A("k_elr", [64, NCH * NH], F32)
        self.AB = 136064
        self.arena = A("arena", [128, self.AB], U8)
        self._aoff = 0


    def seg(self, nbytes, dtype):
        nbytes = (nbytes + 31) & ~31
        t = self.arena[:, self._aoff:self._aoff + nbytes]
        self._aoff += nbytes
        assert self._aoff <= self.AB, f"arena overflow {self._aoff}"
        return t.bitcast(dtype)

    def layout_p01(self):
        self._aoff = 0
        r1_base = self._aoff
        self.a_xbc = [self.seg(1028 * 2, BF16) for _ in range(33)]
        self.a_ubuf = self.arena[:, r1_base:r1_base + 8192].bitcast(F32)
        self.a_sq = self.arena[:, r1_base + 8192:r1_base + 16384].bitcast(F32)
        self.a_xtm = self.arena[:, r1_base:r1_base + 16384].bitcast(BF16)
        self.a_xn0 = self.arena[:, r1_base + 16384:r1_base + 20480].bitcast(BF16)
        self.a_xpp = self.arena[:, r1_base + 16384:r1_base + 32768].bitcast(BF16)
        self.a_xnt = [self.seg(1152 * 2, BF16) for _ in range(KT)]
        self.a_wslab = self.seg(2 * 2048 * 2, BF16)
        self.a_z16 = self.seg(1024 * 2, BF16)
        self.a_cv16a = self.seg(1024 * 2, BF16)
        self.a_conv = self.seg(1024 * 4, F32)
        self.a_small = self.seg(640 * 4, F32)
        self.dq_pk = self.seg(512, U8)
        self.dq_pk16 = self.dq_pk.bitcast(U16)
        self.dq_scx = self.seg(1024 * 2, BF16)
        self.dq_lo = self.seg(512, U8)
        self.dq_hi = self.seg(512, U8)
        self.dq_lo16 = self.dq_lo.bitcast(U16)
        self.dq_hi16 = self.dq_hi.bitcast(U16)
        self.dq_c = self.seg(512 * 2, BF16)
        self.dq_ge = self.seg(512 * 2, BF16)
        self.dq_r = self.seg(512 * 2, BF16)
        self.dq_t = self.seg(512 * 2, BF16)
        self.dq_s = self.seg(512 * 2, BF16)
        self.dq_o = [self.seg(512 * 2, BF16) for _ in range(2)]
        self.p01_end = self._aoff
        assert self._aoff <= self.AB, self._aoff


    def alloc_psum(self):
        self.pb = [self.nc.alloc_psum_tensor(f"pb{i}", [128, 512], F32) for i in range(8)]

    def make_sems(self):
        self.S = {n: S(self.nc, n) for n in SEM_NAMES}

    def add_tap_dram(self, name, src_ap, shape, dtype):
        if name not in self.taps:
            return
        t = self.nc.dram_tensor("tap_" + name, shape, dtype, kind="ExternalOutput")
        self.tap_list.append((t, src_ap))

    def flush_taps(self, blk, wait_sems):
        if not self.tap_list and "outp" not in self.taps:
            pass

        @blk.gpsimd
        def _(g):
            for sem, val in wait_sems:
                g.wait_ge(sem.h, val)
            st = self.S["outw"]
            for t, src in self.tap_list:
                st.inc(g.dma_start(t[:], src), 16)
            g.wait_ge(st.h, st.n)

    def p0(self, blk):
        nc = self.nc
        Ss = self.S

        @blk.gpsimd
        def _(g):
            g.memset(self.c_init[:, 0:16], 0.0)
            g.memset(self.c_init[:, 0:16], 0.0)
            Ss["ini"].inc(g.nop(), 1)
            for i in range(4):
                n4 = NCT * KT * 128 // 4
                Ss["wldA"].inc(g.dma_start(self.ipw16[i * n4:(i + 1) * n4, :],
                                           self.ipw[i * n4:(i + 1) * n4, :]), 16)

        @blk.sync
        def _(sync):
            ld = Ss["cld"]
            ld.inc(sync.dma_start(self.c_idb[:], self.idbf[:]), 16)
            ld.inc(sync.dma_start(self.c_idf[:], self.idf32[:]), 16)
            ld.inc(sync.dma_start(self.c_mask[:, 0:256], self.maskt[0:128, :]), 16)
            ld.inc(sync.dma_start(self.c_mask[:, 256:512], self.maskt[128:256, :]), 16)
            ld.inc(sync.dma_start(self.c_convw[:], self.convw[:]), 16)
            ld.inc(sync.dma_start(self.c_convb[:], self.convb[:]), 16)
            for i in range(32):
                ld.inc(sync.dma_start(self.c_normw[:, i:i + 1], self.normw[128 * i:128 * (i + 1), None]), 16)
            ld.inc(sync.dma_start(self.c_drep[:], self.dskip[0, None, :].broadcast_to((128, NH))), 16)
            ld.inc(sync.dma_start(self.c_ones32[:], self.ones32[:]), 16)
            ld.inc(sync.dma_start(self.c_onesf[:], self.onesf[:]), 16)
            ld.inc(sync.dma_start(self.c_sel9[:], self.sel9[:]), 16)
            ld.inc(sync.dma_start(self.c_acol[:], self.alog[:]), 16)
            ld.inc(sync.dma_start(self.c_dtb[:], self.dtb[:]), 16)
            ld.inc(sync.dma_start(self.c_cst[:], self.constv[:]), 16)

        @blk.scalar
        def _(s):
            s.wait_ge(Ss["ini"].h, 1)
            s.wait_ge(Ss["cld"].h, Ss["cld"].n)
            Ss["aex"].inc(s.activation(self.c_acol[:], self.c_acol[:], AF.Exp), 1)

        @blk.sync
        def _(sync):
            for tt in range(17):
                sync.wait_ge(Ss["cvv"].h, tt)
                if tt == 0:
                    Ss["uld"].inc(sync.dma_start(self.a_ubuf[0:3, :], self.uh[:]), 16)
                else:
                    r0 = 128 * (tt - 1)
                    Ss["uld"].inc(sync.dma_start(self.a_ubuf[:], self.u[r0:r0 + 128, :]), 16)

        A1 = 0

        @blk.vector
        def _(v):
            v.wait_ge(Ss["ini"].h, 1)
            sm = self.a_small
            m, s1, s2, q, var, sd, rstd = (sm[:, i:i + 1] for i in range(7))
            for tt in range(17):
                P = 3 if tt == 0 else 128
                ub = self.a_ubuf[0:P, :]
                v.wait_ge(Ss["uld"].h, 16 * (tt + 1))
                v.reduce_sum(s1[0:P, :], ub, axis=AX.X)
                v.wait_ge(Ss["sil"].h, A1 + 2 * tt + 1)
                Ss["mil"].inc(v.reduce_sum(s2[0:P, :], self.a_sq[0:P, :], axis=AX.X), 1)
                v.wait_ge(Ss["mil"].h, 4 * tt + 1)
                v.tensor_scalar(m[0:P, :], s1[0:P, :], 1.0 / EMBED, None, ALU.mult)
                Ss["mil"].inc(v.tensor_scalar(q[0:P, :], s2[0:P, :], 1.0 / EMBED, None, ALU.mult), 2)
                v.wait_ge(Ss["mil"].h, 4 * tt + 3)
                Ss["mil"].inc(v.tensor_tensor(var[0:P, :], m[0:P, :], m[0:P, :], ALU.mult), 1)
                v.wait_ge(Ss["mil"].h, 4 * tt + 4)
                Ss["dtv"].inc(v.tensor_tensor(var[0:P, :], q[0:P, :], var[0:P, :], ALU.subtract), 1)
                v.wait_ge(Ss["sil"].h, A1 + 2 * tt + 2)
                Ss["dtv"].inc(v.reciprocal(rstd[0:P, :], sd[0:P, :]), 1)
                v.wait_ge(Ss["dtv"].h, 2 * tt + 2)
                v.wait_ge(Ss["xnw"].h, 16 * tt)
                wr = v.tensor_scalar(self.a_xn0[0:P, 0:2048], ub, m[0:P, 0:1], rstd[0:P, 0:1],
                                     ALU.subtract, ALU.mult)
                Ss["cvv"].inc(wr, 1)

        @blk.scalar
        def _(s):
            sm = self.a_small
            for tt in range(17):
                P = 3 if tt == 0 else 128
                s.wait_ge(Ss["uld"].h, 16 * (tt + 1))
                s.wait_ge(Ss["mil"].h, 4 * tt)
                Ss["sil"].inc(s.activation(self.a_sq[0:P, :], self.a_ubuf[0:P, :], AF.Square), 1)
                s.wait_ge(Ss["dtv"].h, 2 * tt + 1)
                Ss["sil"].inc(s.activation(sm[0:P, 5:6], sm[0:P, 4:5], AF.Sqrt, bias=self.c_cst[0:P, 0:1]), 1)

        @blk.gpsimd
        def _(g):
            for tt in range(17):
                P = 3 if tt == 0 else 128
                r0 = 0 if tt == 0 else 3 + 128 * (tt - 1)
                g.wait_ge(Ss["cvv"].h, tt + 1)
                Ss["xnw"].inc(g.dma_start(self.xn_tm[r0:r0 + P, :], self.a_xn0[0:P, 0:2048]), 16)
            g.wait_ge(Ss["xnw"].h, 17 * 16)
            Ss["xnw"].inc(g.nop(), 1)

        self.ln_cvv = Ss["cvv"].n
        self.ln_dtv = Ss["dtv"].n
        self.ln_sil = Ss["sil"].n
        self.ln_mil = Ss["mil"].n

    def p1_half(self, blk, h):
        nc = self.nc
        Ss = self.S
        pb = self.pb
        order = list(range(32, 66)) + list(range(32))
        r0 = 1024 * h
        b_xnt = Ss["xnt"].n
        b_wsl = Ss["wslb"].n
        b_ipe = Ss["ipe"].n
        b_ipv = Ss["ipv"].n
        b_zfw = Ss["zfw"].n
        b_cvv = Ss["cvv"].n
        b_sil = Ss["sil"].n
        b_xfw = Ss["xfw"].n

        @blk.sync
        def _(sync):
            if h == 0:
                sync.wait_ge(Ss["xnw"].h, 17 * 16 + 1)
            else:
                sync.wait_ge(Ss["ipe"].h, b_ipe)
            for k in range(KT):
                Ss["xnt"].inc(sync.dma_start_transpose(
                    self.a_xnt[k][:, 0:1152], self.xn_tm[r0:r0 + 1152, 128 * k:128 * (k + 1)]), 16)
            for idx, ct in enumerate(order):
                if idx >= 2:
                    sync.wait_ge(Ss["ipe"].h, b_ipe + idx - 1)
                par = idx % 2
                rows0 = ct * KT * 128
                Ss["wslb"].inc(sync.dma_start(
                    self.a_wslab[:, 2048 * par:2048 * (par + 1)].rearrange("p (k n) -> p k n", k=KT),
                    self.ipw16[rows0:rows0 + KT * 128, :].rearrange("(k p) n -> p k n", p=128)), 16)

        @blk.tensor
        def _(t):
            if h == 0:
                t.wait_ge(Ss["wldA"].h, 4 * 16)
            t.wait_ge(Ss["xnt"].h, b_xnt + 16 * KT)
            for idx, ct in enumerate(order):
                t.wait_ge(Ss["wslb"].h, b_wsl + 16 * (idx + 1))
                par = idx % 2
                tri = pb[0:3] if par == 0 else pb[3:6]
                if idx >= 2:
                    t.wait_ge(Ss["ipv"].h, b_ipv + idx - 1)
                for k in range(KT):
                    wk = self.a_wslab[:, 2048 * par + 128 * k: 2048 * par + 128 * (k + 1)]
                    t.ldweights(wk)
                    st_, sp = (k == 0), (k == KT - 1)
                    xk = self.a_xnt[k]
                    t.matmul(tri[0][:, 0:512], wk, xk[:, 0:512], start=st_, stop=sp)
                    t.matmul(tri[1][:, 0:512], wk, xk[:, 512:1024], start=st_, stop=sp)
                    mm = t.matmul(tri[2][:, 0:3], wk, xk[:, 1024:1027], start=st_, stop=sp)
                    if sp:
                        Ss["ipe"].inc(mm, 1)

        @blk.scalar
        def _(s):
            for idx, ct in enumerate(order):
                s.wait_ge(Ss["ipe"].h, b_ipe + idx + 1)
                par = idx % 2
                tri = pb[0:3] if par == 0 else pb[3:6]
                if ct == 65:
                    dst = self.k_dtraw
                    s.copy(dst[:, 0:512], tri[0][0:32, 0:512])
                    s.copy(dst[:, 512:1024], tri[1][0:32, 0:512])
                    cp = s.copy(dst[:, 1024:1027], tri[2][0:32, 0:3])
                elif ct >= 32:
                    dst = self.a_xbc[ct - 32]
                    s.copy(dst[:, 0:512], tri[0][:, 0:512])
                    s.copy(dst[:, 512:1024], tri[1][:, 0:512])
                    cp = s.copy(dst[:, 1024:1027], tri[2][:, 0:3])
                else:
                    zi = idx - 34
                    if zi >= 1:
                        s.wait_ge(Ss["zfw"].h, b_zfw + 16 * zi)
                    buf = self.a_z16
                    s.copy(buf[:, 0:509], tri[0][:, 3:512])
                    s.copy(buf[:, 509:1021], tri[1][:, 0:512])
                    cp = s.copy(buf[:, 1021:1024], tri[2][:, 0:3])
                Ss["ipv"].inc(cp, 1)

        @blk.gpsimd
        def _(g):
            for zi in range(32):
                idx = 34 + zi
                g.wait_ge(Ss["ipv"].h, b_ipv + idx + 1)
                Ss["zfw"].inc(g.dma_start(
                    self.z_fm[128 * zi:128 * (zi + 1), r0:r0 + 1024], self.a_z16[:, 0:1024]), 16)

        @blk.vector
        def _(v):
            v.wait_ge(Ss["cld"].h, Ss["cld"].n)
            for c in range(33):
                v.wait_ge(Ss["ipv"].h, b_ipv + c + 1)
                if c >= 1:
                    v.wait_ge(Ss["sil"].h, b_sil + c)
                xb = self.a_xbc[c]
                w = self.c_convw
                cv = self.a_conv[:, 0:1024]
                v.tensor_scalar(cv, xb[:, 0:1024], w[:, 4 * c:4 * c + 1], None, ALU.mult)
                v.scalar_tensor_tensor(cv, xb[:, 1:1025], w[:, 4 * c + 1:4 * c + 2], cv, ALU.mult, ALU.add)
                v.scalar_tensor_tensor(cv, xb[:, 2:1026], w[:, 4 * c + 2:4 * c + 3], cv, ALU.mult, ALU.add)
                cc_ = v.scalar_tensor_tensor(cv, xb[:, 3:1027], w[:, 4 * c + 3:4 * c + 4], cv, ALU.mult, ALU.add)
                Ss["cvv"].inc(cc_, 1)

        @blk.scalar
        def _(s):
            for c in range(33):
                s.wait_ge(Ss["cvv"].h, b_cvv + c + 1)
                if c < 32:
                    if c >= 1:
                        s.wait_ge(Ss["xfw"].h, b_xfw + 16 * c)
                    ac = s.activation(self.a_cv16a[:, 0:1024], self.a_conv[:, 0:1024], AF.Silu,
                                      bias=self.c_convb[:, c:c + 1])
                else:
                    ac = s.activation(self.k_bc[:, 1024 * h:1024 * (h + 1)], self.a_conv[:, 0:1024],
                                      AF.Silu, bias=self.c_convb[:, c:c + 1])
                Ss["sil"].inc(ac, 1)

        @blk.gpsimd
        def _(g):
            for c in range(32):
                g.wait_ge(Ss["sil"].h, b_sil + c + 1)
                Ss["xfw"].inc(g.dma_start(
                    self.x_fm[128 * c:128 * (c + 1), r0:r0 + 1024], self.a_cv16a[:, 0:1024]), 16)

        self.conv_sil_done[h] = Ss["sil"].n

        @blk.scalar
        def _(s):
            t1, t2 = self.k_t1, self.k_t2
            dtr = self.k_dtraw[:, 3:1027]
            s.wait_ge(Ss["ipv"].h, b_ipv + 34)
            s.activation(t1, dtr, AF.Abs, bias=self.c_dtb[:, 0:1])
            s.activation(t1, t1, AF.Exp, scale=-1.0)
            s.activation(t1, t1, AF.Ln, bias=1.0)
            ac = s.activation(t2, dtr, AF.Relu, bias=self.c_dtb[:, 0:1])
            Ss["sil"].inc(ac, 1)

        sil_dt = Ss["sil"].n

        @blk.vector
        def _(v):
            dA = self.k_dA
            v.wait_ge(Ss["sil"].h, sil_dt)
            v.wait_ge(Ss["aex"].h, 1)
            dtp = self.k_dtpos[:, r0:r0 + 1024]
            v.tensor_tensor(dtp, self.k_t1, self.k_t2, ALU.add)
            da_ = v.tensor_scalar(dA, dtp, self.c_acol[0:32, 0:1], -1.0, ALU.mult, ALU.mult)
            Ss["dtv"].inc(da_, 1)
            v.wait_ge(Ss["dtv"].h, Ss["dtv"].n)
            for lc in range(4):
                cc = 4 * h + lc
                sc_ = v.tensor_tensor_scan(
                    self.k_a_fm[:, 256 * cc:256 * (cc + 1)], self.c_ones32[:, 0:256],
                    dA[:, 256 * lc:256 * (lc + 1)], 0.0, ALU.mult, ALU.add)
                Ss["dtv"].inc(sc_, 1)
            v.wait_ge(Ss["dtv"].h, Ss["dtv"].n)
            for lc in range(4):
                cc = 4 * h + lc
                alast = self.k_a_fm[:, 256 * cc + 255:256 * cc + 256]
                if cc == 0:
                    Ss["dtv"].inc(v.tensor_copy(self.c_asum, alast), 1)
                else:
                    Ss["dtv"].inc(v.tensor_tensor(self.c_asum, self.c_asum, alast, ALU.add), 1)
                v.wait_ge(Ss["dtv"].h, Ss["dtv"].n)
            Ss["mil"].inc(v.nop(), 1)

        for lc in range(4):
            self.p1_one_chunk(blk, h, 4 * h + lc)

    def p1_one_chunk(self, blk, h, cc):
        nc = self.nc
        Ss = self.S
        pb = self.pb
        afm = self.k_a_fm[:, 256 * cc:256 * (cc + 1)]
        dtp = self.k_dtpos[:, 256 * cc:256 * (cc + 1)]
        d2e, de2f = self.k_d2e, self.k_de2f
        arow = self.a_small[0:1, 512:544]
        erow = self.a_small[0:1, 544:576]
        mile = Ss["mil"].n

        b_afw = Ss["afw"].n
        b_arw = Ss["arw"].n
        b_aex = Ss["aex"].n
        b_pet = Ss["pet"].n
        b_scv = Ss["scv"].n
        b_xtl = Ss["xtl"].n
        b_xpv = Ss["xpv"].n
        b_stm = Ss["stm"].n
        b_sue = Ss["sue"].n
        b_suw = Ss["suw"].n

        @blk.gpsimd
        def _(g):
            g.wait_ge(Ss["mil"].h, mile)
            Ss["afw"].inc(g.dma_start(
                self.aflat[cc, :].rearrange("(hh t) -> hh t", hh=NH), afm), 16)

        @blk.sync
        def _(sync):
            sync.wait_ge(Ss["afw"].h, b_afw + 16)
            sync.wait_ge(Ss["aex"].h, b_aex)
            Ss["arw"].inc(sync.dma_start(
                arow,
                self.aflat[cc, :].rearrange("(hh t) -> hh t", hh=NH)[:, 255][None, :],
                allow_slow_non_contiguous=True), 16)
            sync.wait_ge(Ss["xfw"].h, 16 * 32 * (h + 1))
            sync.wait_ge(Ss["stm"].h, b_stm)
            sync.wait_ge(Ss["xpv"].h, b_xpv)
            for tt in range(2):
                c0 = 256 * cc + 128 * tt
                Ss["xtl"].inc(sync.dma_start_transpose(
                    self.a_xtm[:, 4096 * tt:4096 * (tt + 1)], self.x_fm[:, c0:c0 + 128]), 16)

        @blk.scalar
        def _(s):
            s.wait_ge(Ss["mil"].h, mile)
            s.wait_ge(Ss["scv"].h, b_scv)
            Ss["aex"].inc(s.activation(d2e, afm, AF.Exp, scale=-1.0, bias=afm[:, 255:256]), 1)
            s.wait_ge(Ss["arw"].h, b_arw + 16)
            s.wait_ge(Ss["pet"].h, b_pet)
            Ss["aex"].inc(s.activation(erow, arow, AF.Exp), 1)

        @blk.vector
        def _(v):
            v.wait_ge(Ss["aex"].h, b_aex + 1)
            v.wait_ge(Ss["pet"].h, b_pet)
            Ss["scv"].inc(v.tensor_tensor(de2f, dtp, d2e, ALU.mult), 1)

        tsrc = [afm[:, 0:128], afm[:, 128:256],
                dtp[:, 0:128], dtp[:, 128:256],
                de2f[:, 0:128], de2f[:, 128:256]]

        @blk.tensor
        def _(t):
            t.wait_ge(Ss["mil"].h, mile)
            for i in range(6):
                bank = pb[6 + i % 2]
                if i >= 4:
                    t.wait_ge(Ss["scv"].h, b_scv + 1)
                if i >= 2:
                    t.wait_ge(Ss["sue"].h, b_sue + i - 1)
                tr = t.transpose(bank[:, 0:32], tsrc[i], self.c_idf[0:32, 0:32])
                Ss["pet"].inc(tr, 1)
            t.wait_ge(Ss["aex"].h, b_aex + 2)
            t.wait_ge(Ss["sue"].h, b_sue + 5)
            mmr = t.matmul(pb[6][0:64, 256:288], self.c_onesf[0:1, 0:64], erow, start=True, stop=True)
            Ss["pet"].inc(mmr, 1)
            ptb = pb[7][:, 32:64].bitcast(BF16)
            t.wait_ge(Ss["sil"].h, self.conv_sil_done[h])
            t.wait_ge(Ss["sue"].h, b_sue + 6)
            tr = t.transpose(ptb, self.k_bc[0:64, 256 * cc:256 * cc + 128], self.c_idb[0:64, 0:64])
            Ss["pet"].inc(tr, 1)
            t.wait_ge(Ss["sue"].h, b_sue + 8)
            tr = t.transpose(ptb, self.k_bc[0:64, 256 * cc + 128:256 * cc + 256], self.c_idb[0:64, 0:64])
            Ss["pet"].inc(tr, 1)
            t.wait_ge(Ss["xpv"].h, b_xpv + 1)
            t.wait_ge(Ss["sue"].h, b_sue + 9)
            for g8 in range(8):
                bank = pb[4] if g8 % 2 == 0 else pb[5]
                if g8 >= 2:
                    t.wait_ge(Ss["sue"].h, b_sue + 8 + g8)
                for tt in range(2):
                    mm = t.matmul(bank[0:64, 0:512],
                                  self.k_btm[:, 64 * tt:64 * (tt + 1)],
                                  self.a_xpp[:, 4096 * tt + 512 * g8:4096 * tt + 512 * (g8 + 1)],
                                  start=(tt == 0), stop=(tt == 1))
                Ss["stm"].inc(mm, 1)

        @blk.vector
        def _(v):
            dst6 = [self.k_atm[:, 64 * cc:64 * cc + 32], self.k_atm[:, 64 * cc + 32:64 * cc + 64],
                    self.k_dttm[:, 64 * cc:64 * cc + 32], self.k_dttm[:, 64 * cc + 32:64 * cc + 64],
                    self.k_de2[:, 0:32], self.k_de2[:, 32:64]]
            for i in range(6):
                bank = pb[6 + i % 2]
                v.wait_ge(Ss["pet"].h, b_pet + i + 1)
                Ss["sue"].inc(v.tensor_copy(dst6[i], bank[:, 0:32]), 1)
            v.wait_ge(Ss["pet"].h, b_pet + 7)
            Ss["sue"].inc(v.tensor_copy(self.k_elr[:, 32 * cc:32 * (cc + 1)], pb[6][0:64, 256:288]), 1)
            ptb = pb[7][:, 32:64].bitcast(BF16)
            v.wait_ge(Ss["pet"].h, b_pet + 8)
            Ss["sue"].inc(v.tensor_copy(self.k_btm[:, 0:64], ptb), 1)
            v.wait_ge(Ss["pet"].h, b_pet + 9)
            Ss["sue"].inc(v.tensor_copy(self.k_btm[:, 64:128], ptb), 1)
            v.wait_ge(Ss["xtl"].h, b_xtl + 32)
            v.wait_ge(Ss["sue"].h, b_sue + 9)
            op = None
            for tt in range(2):
                for hh in range(NH):
                    sl = slice(4096 * tt + 128 * hh, 4096 * tt + 128 * (hh + 1))
                    op = v.tensor_scalar(self.a_xpp[:, sl], self.a_xtm[:, sl],
                                         self.k_de2[:, 32 * tt + hh:32 * tt + hh + 1], None, ALU.mult)
            Ss["xpv"].inc(op, 1)
            for g8 in range(8):
                bank = pb[4] if g8 % 2 == 0 else pb[5]
                v.wait_ge(Ss["stm"].h, b_stm + g8 + 1)
                if g8 == 0:
                    v.wait_ge(Ss["suw"].h, b_suw)
                v.tensor_copy(self.s_su[:, 512 * g8:512 * (g8 + 1)], bank[0:64, 0:512])
                if cc == 0:
                    cp = v.tensor_copy(self.s_loc[:, 512 * g8:512 * (g8 + 1)], bank[0:64, 0:512])
                else:
                    cp = v.scalar_tensor_tensor(self.s_loc[:, 512 * g8:512 * (g8 + 1)],
                                                bank[0:64, 0:512], 1.0,
                                                self.s_loc[:, 512 * g8:512 * (g8 + 1)],
                                                ALU.mult, ALU.add)
                Ss["sue"].inc(cp, 1)
            v.wait_ge(Ss["sue"].h, b_sue + 17)
            op = None
            for hh in range(NH):
                op = v.tensor_scalar(self.s_loc[:, 128 * hh:128 * (hh + 1)],
                                     self.s_loc[:, 128 * hh:128 * (hh + 1)],
                                     self.k_elr[:, 32 * cc + hh:32 * cc + hh + 1], None, ALU.mult)
            Ss["scv"].inc(op, 1)

        @blk.scalar
        def _(s):
            s.wait_ge(Ss["sue"].h, b_sue + 2)
            Ss["aex"].inc(s.activation(self.k_etm[:, 64 * cc:64 * (cc + 1)],
                                       self.k_atm[:, 64 * cc:64 * (cc + 1)], AF.Exp), 1)

        @blk.gpsimd
        def _(g):
            g.wait_ge(Ss["sue"].h, b_sue + 17)
            Ss["suw"].inc(g.dma_start(self.su_dram[64 * cc:64 * (cc + 1), :], self.s_su), 16)

    def layout_p2(self):
        r1 = 0
        self.q_xtm = self.arena[:, r1:r1 + 16384].bitcast(BF16)
        self.q_xp = self.arena[:, r1 + 16384:r1 + 32768].bitcast(BF16)
        self.q_y0 = self.arena[:, r1 + 32768:r1 + 49152].bitcast(BF16)
        self.q_zyn = self.arena[:, r1 + 49152:r1 + 65536].bitcast(BF16)
        r2 = 33 * 2056 // 32 * 32
        r2 = (33 * 2056 + 31) & ~31
        self.q_sig = self.arena[:, r2:r2 + 16384].bitcast(F32)
        self.q_su2 = self.arena[:, r2 + 16384:r2 + 32768].bitcast(F32)[0:64, :]
        r3 = r2 + 36864
        self.q_mt = self.arena[:, r3:r3 + 4096].bitcast(BF16)
        self.q_sct = self.arena[:, r3 + 4096:r3 + 5120].bitcast(BF16)
        self.q_seg = self.arena[:, r3 + 5120:r3 + 9216].bitcast(F32)
        self.q_es = self.arena[:, r3 + 9216:r3 + 11264].bitcast(BF16)
        self.q_dec = self.arena[:, r3 + 11264:r3 + 12288].bitcast(F32)[0:64, :]
        self.q_as9 = self.arena[:, r3 + 12288:r3 + 12416].bitcast(F32)[0:9, 0:32]
        self.q_ccp = self.arena[:, r3 + 12416:r3 + 12928].bitcast(BF16)[0:64, :]
        self.q_arow = self.k32[0:1, 0:8192]

    def p2_pre(self, blk):
        nc = self.nc
        Ss = self.S
        pb = self.pb
        b_gtl = Ss["gtl"].n
        scv_all = Ss["scv"].n
        suw_all = Ss["suw"].n

        @blk.gpsimd
        def _(g):
            g.wait_ge(Ss["scv"].h, scv_all)
            g.wait_ge(Ss["suw"].h, suw_all)
            Ss["gtl"].inc(g.dma_start(self.cinS[0:64, :], self.s_loc), 16)
            Ss["gtl"].inc(g.dma_start(self.asd[0, :][:, None], self.c_asum), 16)
            g.wait_ge(Ss["gtl"].h, b_gtl + 32)
            Ss["gtl"].inc(g.dma_start(self.cinS[64:65, 0:NH], self.asd[:]), 16)
            g.wait_ge(Ss["gtl"].h, b_gtl + 48)
            g.collective_compute(
                "AllGather", mybir.AluOpType.bypass,
                replica_groups=[list(range(NCORE))],
                ins=[self.cinS.ap().opt()],
                outs=[self.gathS.ap().opt()],
            ).then_inc(Ss["ccs"].h, 1)
            Ss["ccs"].n += 1
            g.wait_ge(Ss["ccs"].h, 1)
            for j in range(NCORE):
                Ss["gtl"].inc(g.dma_start(self.q_as9[j:j + 1, :], self.gathS[65 * j + 64:65 * j + 65, 0:NH]), 16)
            Ss["gtl"].inc(g.dma_start(self.q_as9[8:9, :], self.onesf[0:1, 0:NH]), 16)

        b_g2 = Ss["gtl"].n

        @blk.tensor
        def _(t):
            t.wait_ge(Ss["gtl"].h, b_g2)
            mm = t.matmul(pb[5][0:8, 0:32], self.c_sel9[:], self.q_as9[:], start=True, stop=True)
            Ss["pet"].inc(mm, 1)

        b_p5 = Ss["pet"].n
        b_sue = Ss["sue"].n

        @blk.vector
        def _(v):
            v.wait_ge(Ss["pet"].h, b_p5)
            Ss["sue"].inc(v.tensor_copy(self.a_small[0:8, 0:32], pb[5][0:8, 0:32]), 1)

        @blk.gpsimd
        def _(g):
            g.wait_ge(Ss["sue"].h, b_sue + 1)
            Ss["gtl"].inc(g.dma_start(self.decd[0, :].rearrange("(a b) -> a b", a=8), self.a_small[0:8, 0:32]), 16)

        b_g3 = Ss["gtl"].n

        @blk.sync
        def _(sync):
            sync.wait_ge(Ss["gtl"].h, b_g3)
            Ss["arw"].inc(sync.dma_start(self.a_small[0:1, 64:320], self.decd[:]), 16)

        b_a2 = Ss["arw"].n

        @blk.tensor
        def _(t):
            t.wait_ge(Ss["arw"].h, b_a2)
            mm = t.matmul(pb[5][0:64, 64:320], self.c_onesf[0:1, 0:64], self.a_small[0:1, 64:320],
                          start=True, stop=True)
            Ss["pet"].inc(mm, 1)

        b_p6 = Ss["pet"].n

        @blk.scalar
        def _(s):
            s.wait_ge(Ss["pet"].h, b_p6)
            Ss["aex"].inc(s.activation(self.q_dec[:], pb[5][0:64, 64:320], AF.Exp), 1)

        b_ax = Ss["aex"].n
        b_su3 = Ss["sue"].n

        @blk.sync
        def _(sync):
            for j in range(NCORE):
                if j >= 1:
                    sync.wait_ge(Ss["sin"].h, j)
                Ss["p2a"].inc(sync.dma_start(self.q_su2[:], self.gathS[65 * j:65 * j + 64, :]), 16)

        @blk.vector
        def _(v):
            v.wait_ge(Ss["aex"].h, b_ax)
            for j in range(NCORE):
                v.wait_ge(Ss["p2a"].h, 16 * (j + 1))
                op = None
                for hh in range(NH):
                    sl = slice(128 * hh, 128 * (hh + 1))
                    sc_ = self.q_dec[:, 32 * j + hh:32 * j + hh + 1]
                    if j == 0:
                        op = v.tensor_scalar(self.s_loc[:, sl], self.q_su2[:, sl], sc_, None, ALU.mult)
                    else:
                        op = v.scalar_tensor_tensor(self.s_loc[:, sl], self.q_su2[:, sl], sc_,
                                                    self.s_loc[:, sl], ALU.mult, ALU.add)
                Ss["sin"].inc(op, 1)

    def p2_chunk(self, blk, cc):
        nc = self.nc
        Ss = self.S
        pb = self.pb
        b_p2a = Ss["p2a"].n
        b_p2b = Ss["p2b"].n
        b_p2c = Ss["p2c"].n
        b_p2d = Ss["p2d"].n
        b_p2e = Ss["p2e"].n
        b_p2f = Ss["p2f"].n
        b_aex = Ss["aex"].n
        b_opw = Ss["opw"].n
        b_sin = Ss["sin"].n
        b_scv = Ss["scv"].n

        @blk.sync
        def _(sync):
            if cc == 0:
                sync.wait_ge(Ss["sin"].h, b_sin)
            sync.wait_ge(Ss["p2b"].h, b_p2b)
            sync.wait_ge(Ss["p2e"].h, b_p2e)
            for tt in range(2):
                c0 = 256 * cc + 128 * tt
                Ss["p2a"].inc(sync.dma_start_transpose(
                    self.q_xtm[:, 4096 * tt:4096 * (tt + 1)], self.x_fm[:, c0:c0 + 128]), 16)
            Ss["p2a"].inc(sync.dma_start(self.q_su2[:], self.su_dram[64 * cc:64 * (cc + 1), :]), 16)
            Ss["p2a"].inc(sync.dma_start(self.q_arow, self.aflat[cc, :][None, :]), 16)
            Ss["p2a"].inc(sync.dma_start_transpose(
                self.q_zyn[:, 0:4096], self.z_fm[:, 256 * cc:256 * cc + 128]), 16)

        ld0 = b_p2a + 16 * 5

        @blk.vector
        def _(v):
            v.wait_ge(Ss["scv"].h, b_scv)
            v.wait_ge(Ss["p2c"].h, b_p2c)
            v.tensor_copy(self.q_ccp, self.k_bc[64:128, 256 * cc:256 * (cc + 1)])
            Ss["p2b"].inc(v.tensor_copy(self.s_p16, self.s_loc), 1)
            v.wait_ge(Ss["p2a"].h, b_p2a + 16 * 3)
            op = None
            for tt in range(2):
                for hh in range(NH):
                    sl = slice(4096 * tt + 128 * hh, 4096 * tt + 128 * (hh + 1))
                    op = v.tensor_scalar(self.q_xp[:, sl], self.q_xtm[:, sl],
                                         self.k_dttm[:, 64 * cc + 32 * tt + hh:64 * cc + 32 * tt + hh + 1],
                                         None, ALU.mult)
            Ss["p2b"].inc(op, 1)

        @blk.tensor
        def _(t):
            t.wait_ge(Ss["sil"].h, self.conv_sil_done[cc // 4])
            t.wait_ge(Ss["p2b"].h, b_p2b + 1)
            t.wait_ge(Ss["p2d"].h, b_p2d)
            mm = None
            for jt in range(2):
                mm = t.matmul(pb[2][:, 256 * jt:256 * (jt + 1)],
                              self.k_bc[0:64, 256 * cc + 128 * jt:256 * cc + 128 * (jt + 1)],
                              self.q_ccp[:, :],
                              start=True, stop=True)
            Ss["p2c"].inc(mm, 1)

        @blk.vector
        def _(v):
            v.wait_ge(Ss["p2c"].h, b_p2c + 1)
            Ss["p2d"].inc(v.tensor_copy(self.q_sct[:], pb[2][:, 0:512]), 1)

        pd_pos = {}
        pos = b_p2d + 1
        for r in range(33):
            if r >= 1:
                pos += 1
                pd_pos[("mt", r - 1)] = pos
                if (r - 1) % 4 == 3:
                    g = (r - 1) // 4
                    for it in range(2):
                        pos += 1
                        pd_pos[("ev", 2 * g + it)] = pos

        def pc_R(r):
            g, hl = divmod(r, 4)
            return b_p2c + 1 + 6 * g + hl + 1

        def pc_Y(y):
            g, it = divmod(y, 2)
            return b_p2c + 1 + 6 * g + 4 + it + 1

        @blk.tensor
        def _(t):
            t.wait_ge(Ss["p2a"].h, b_p2a + 16 * 4)
            for g in range(8):
                for hl in range(4):
                    r = 4 * g + hl
                    hh = 4 * g + hl
                    bank = pb[hl % 2]
                    if r >= 2:
                        t.wait_ge(Ss["p2f"].h, b_p2f + r - 1)
                    mm = t.matmul(bank[:, 0:256], self.c_onesf[0:1, :],
                                  self.q_arow[0:1, 256 * hh:256 * (hh + 1)], start=True, stop=True)
                    Ss["p2c"].inc(mm, 1)
                t.wait_ge(Ss["p2d"].h, pd_pos[("mt", 4 * g + 3)])
                if g == 0:
                    t.wait_ge(Ss["p2b"].h, b_p2b + 2)
                for it in range(2):
                    y = 2 * g + it
                    ybank = pb[3 + y % 2]
                    if y >= 2:
                        t.wait_ge(Ss["p2d"].h, pd_pos[("ev", y - 2)])
                    mm = t.matmul(ybank[:, 0:512],
                                  self.q_ccp[:, 128 * it:128 * (it + 1)],
                                  self.s_p16[:, 512 * g:512 * (g + 1)],
                                  start=True, stop=False)
                    last = None
                    for hl in range(4):
                        hh = 4 * g + hl
                        for jt in range(it + 1):
                            sp = (hl == 3 and jt == it)
                            last = t.matmul(ybank[:, 128 * hl:128 * (hl + 1)],
                                            self.q_mt[:, 512 * hl + 256 * jt + 128 * it:512 * hl + 256 * jt + 128 * (it + 1)],
                                            self.q_xp[:, 4096 * jt + 128 * hh:4096 * jt + 128 * (hh + 1)],
                                            start=False, stop=sp)
                    Ss["p2c"].inc(last, 1)

        @blk.scalar
        def _(s):
            for r in range(32):
                s.wait_ge(Ss["p2f"].h, b_p2f + r + 1)
                if r >= 2:
                    s.wait_ge(Ss["p2d"].h, pd_pos[("mt", r - 2)])
                par = r % 2
                op = None
                for jt in range(2):
                    op = s.activation(self.q_es[:, 512 * par + 256 * jt:512 * par + 256 * (jt + 1)],
                                      self.q_seg[:, 512 * par + 256 * jt:512 * par + 256 * (jt + 1)],
                                      AF.Exp)
                Ss["aex"].inc(op, 1)

        @blk.vector
        def _(v):
            v.wait_ge(Ss["p2c"].h, b_p2c + 1)
            Ss["p2d"].inc(v.tensor_copy(self.q_sct[:], pb[2][:, 0:512]), 1)
            for r in range(33):
                if r < 32:
                    g, hl = divmod(r, 4)
                    hh = r
                    bank = pb[hl % 2]
                    par = r % 2
                    v.wait_ge(Ss["p2c"].h, pc_R(r))
                    if r >= 2:
                        v.wait_ge(Ss["aex"].h, b_aex + r - 1)
                    op = None
                    for jt in range(2):
                        acol = self.k_atm[:, 64 * cc + 32 * jt + hh:64 * cc + 32 * jt + hh + 1]
                        op = v.scalar_tensor_tensor(
                            self.q_seg[:, 512 * par + 256 * jt:512 * par + 256 * (jt + 1)],
                            bank[:, 0:256], acol,
                            self.c_mask[:, 256 * jt:256 * (jt + 1)],
                            ALU.subtract, ALU.add)
                    Ss["p2f"].inc(op, 1)
                if r >= 1:
                    rr = r - 1
                    par = rr % 2
                    hl = rr % 4
                    v.wait_ge(Ss["aex"].h, b_aex + rr + 1)
                    op = None
                    for jt in range(2):
                        op = v.tensor_tensor(
                            self.q_mt[:, 512 * hl + 256 * jt:512 * hl + 256 * (jt + 1)],
                            self.q_es[:, 512 * par + 256 * jt:512 * par + 256 * (jt + 1)],
                            self.q_sct[:, 256 * jt:256 * (jt + 1)], ALU.mult)
                    Ss["p2d"].inc(op, 1)
                    if rr % 4 == 3:
                        g = rr // 4
                        for it in range(2):
                            y = 2 * g + it
                            ybank = pb[3 + y % 2]
                            v.wait_ge(Ss["p2c"].h, pc_Y(y))
                            op = None
                            for hl2 in range(4):
                                hh2 = 4 * g + hl2
                                ecol = self.k_etm[:, 64 * cc + 32 * it + hh2:64 * cc + 32 * it + hh2 + 1]
                                op = v.tensor_scalar(
                                    self.q_y0[:, 4096 * it + 512 * g + 128 * hl2:4096 * it + 512 * g + 128 * (hl2 + 1)],
                                    ybank[:, 128 * hl2:128 * (hl2 + 1)], ecol, None, ALU.mult)
                            Ss["p2d"].inc(op, 1)

        b_ax2 = b_aex + 32
        rsum = self.a_small[0:128, 320:352]
        rstd = self.a_small[0:128, 352:384]
        rs2 = self.a_small[0:128, 384:416]

        @blk.sync
        def _(sync):
            sync.wait_ge(Ss["p2e"].h, b_p2e + 1)
            Ss["p2a"].inc(sync.dma_start_transpose(
                self.q_zyn[:, 0:4096], self.z_fm[:, 256 * cc + 128:256 * cc + 256]), 16)

        @blk.scalar
        def _(s):
            for it in range(2):
                s.wait_ge(Ss["p2a"].h, (ld0 if it == 0 else b_p2a + 16 * 6))
                if it == 1:
                    s.wait_ge(Ss["p2f"].h, b_p2f + 35)
                op = s.activation(self.q_sig[:], self.q_zyn[:, 0:4096], AF.Sigmoid)
                Ss["p2e"].inc(op, 1)
                s.wait_ge(Ss["p2f"].h, b_p2f + 34 + 3 * it)
                Ss["aex"].inc(s.activation(self.q_sig[:], self.q_y0[:, 4096 * it:4096 * (it + 1)], AF.Square), 1)
                s.wait_ge(Ss["p2f"].h, b_p2f + 35 + 3 * it)
                s.activation(rs2, rsum, AF.Ln, scale=1.0 / 128.0, bias=self.c_cst[:, 1:2])
                Ss["aex"].inc(s.activation(rstd, rs2, AF.Exp, scale=-0.5), 1)

        @blk.vector
        def _(v):
            for it in range(2):
                v.wait_ge(Ss["p2d"].h, pd_pos[("ev", 14 + it)])
                op = None
                for hh in range(NH):
                    sl = slice(4096 * it + 128 * hh, 4096 * it + 128 * (hh + 1))
                    op = v.scalar_tensor_tensor(self.q_y0[:, sl], self.q_xtm[:, sl],
                                                self.c_drep[:, hh:hh + 1], self.q_y0[:, sl],
                                                ALU.mult, ALU.add)
                Ss["p2f"].inc(op, 1)
                v.wait_ge(Ss["p2e"].h, b_p2e + 1 + 2 * it)
                y0it = self.q_y0[:, 4096 * it:4096 * (it + 1)]
                Ss["p2f"].inc(v.tensor_tensor(y0it, y0it, self.q_sig[:], ALU.mult), 1)
                v.wait_ge(Ss["aex"].h, b_ax2 + 1 + 2 * it)
                Ss["p2f"].inc(v.reduce_sum(
                    rsum, self.q_sig[:, 0:4096].rearrange("p (h q) -> p h q", h=NH), axis=AX.X), 1)
                v.wait_ge(Ss["aex"].h, b_ax2 + 2 + 2 * it)
                v.wait_ge(Ss["opw"].h, b_opw + 16 * it)
                op = None
                for hh in range(NH):
                    sl = slice(4096 * it + 128 * hh, 4096 * it + 128 * (hh + 1))
                    op = v.tensor_scalar(self.q_zyn[:, 4096 + 128 * hh:4096 + 128 * (hh + 1)],
                                         self.q_y0[:, sl], rstd[:, hh:hh + 1], None, ALU.mult)
                Ss["p2e"].inc(op, 1)

        @blk.gpsimd
        def _(g):
            for it in range(2):
                g.wait_ge(Ss["p2e"].h, b_p2e + 2 + 2 * it)
                r0 = 256 * cc + 128 * it
                Ss["opw"].inc(g.dma_start(self.yn_tm[r0:r0 + 128, :], self.q_zyn[:, 4096:8192]), 16)

        @blk.vector
        def _(v):
            v.wait_ge(Ss["p2a"].h, b_p2a + 16 * 3)
            v.wait_ge(Ss["p2c"].h, pc_Y(15))
            op = None
            for hh in range(NH):
                sl = slice(128 * hh, 128 * (hh + 1))
                el = self.k_elr[:, 32 * cc + hh:32 * cc + hh + 1]
                v.tensor_scalar(self.s_loc[:, sl], self.s_loc[:, sl], el, None, ALU.mult)
                op = v.scalar_tensor_tensor(self.s_loc[:, sl], self.q_su2[:, sl], el,
                                            self.s_loc[:, sl], ALU.mult, ALU.add)
            Ss["scv"].inc(op, 1)
            Ss["p2b"].inc(v.nop(), 1)

    def p0_ow(self, blk):
        Ss = self.S
        b_uld = Ss["uld"].n
        b_dtv = Ss["dtv"].n
        b_oww = Ss["opv"].n

        @blk.sync
        def _(sync):
            sync.wait_ge(Ss["cvv"].h, self.ln_cvv)
            for i in range(32):
                if i >= 1:
                    sync.wait_ge(Ss["dtv"].h, b_dtv + i)
                Ss["uld"].inc(sync.dma_start(self.a_ubuf[:], self.oww[128 * i:128 * (i + 1), :]), 16)

        @blk.vector
        def _(v):
            v.wait_ge(Ss["xnw"].h, 17 * 16 + 1)
            for i in range(32):
                v.wait_ge(Ss["uld"].h, b_uld + 16 * (i + 1))
                if i >= 1:
                    v.wait_ge(Ss["opv"].h, b_oww + 16 * i)
                op = v.tensor_scalar(self.a_xn0[:, 0:2048], self.a_ubuf[:],
                                     self.c_normw[:, i:i + 1], None, ALU.mult)
                Ss["dtv"].inc(op, 1)

        @blk.gpsimd
        def _(g):
            for i in range(32):
                g.wait_ge(Ss["dtv"].h, b_dtv + i + 1)
                Ss["opv"].inc(g.dma_start(self.ow16[128 * i:128 * (i + 1), :], self.a_xn0[:, 0:2048]), 16)

    def dq_init(self):
        self.dq_units = [(m, i, ch) for m in range(3) for i in range(8) for ch in range(4)]
        self.dq_done = 0
        self.dq_ffa_base = self.S["ffa"].n

    def dequant_piece(self, blk, n_units):
        Ss = self.S
        lo = self.dq_done
        hi = min(lo + n_units, len(self.dq_units))
        self.dq_done = hi
        if lo >= hi:
            return
        pks = [self.pkg, self.pku, self.pkd]
        scs = [self.scg, self.scu, self.scd]

        @blk.sync
        def _(sync):
            for idx in range(lo, hi):
                m, i, ch = self.dq_units[idx]
                if idx >= 1:
                    sync.wait_ge(Ss["dqv"].h, idx)
                Ss["ffa"].inc(sync.dma_start(
                    self.dq_pk[:, :], pks[m][128 * i:128 * (i + 1), 512 * ch:512 * (ch + 1)]), 16)


        @blk.vector
        def _(v):
            for idx in range(lo, hi):
                m, i, ch = self.dq_units[idx]
                v.wait_ge(Ss["ffa"].h, self.dq_ffa_target(idx))
                if idx >= 1:
                    v.wait_ge(Ss["dqw"].h, 32 * idx)
                v.tensor_scalar(self.dq_hi16[:], self.dq_pk16[:], 4, None, ALU.logical_shift_right)
                v.tensor_scalar(self.dq_hi16[:], self.dq_hi16[:], 0x0F0F, None, ALU.bitwise_and)
                v.tensor_scalar(self.dq_lo16[:], self.dq_pk16[:], 0x0F0F, None, ALU.bitwise_and)
                op = None
                for half, src8 in ((0, self.dq_lo), (1, self.dq_hi)):
                    c, ge, r, t, sgn = self.dq_c, self.dq_ge, self.dq_r, self.dq_t, self.dq_s
                    v.tensor_copy(c[:], src8[:])
                    v.tensor_scalar(ge[:], c[:], 8.0, None, ALU.is_ge)
                    v.scalar_tensor_tensor(r[:], ge[:], -8.0, c[:], ALU.mult, ALU.add)
                    v.tensor_scalar(sgn[:], ge[:], -2.0, 1.0, ALU.mult, ALU.add)
                    v.tensor_scalar(t[:], r[:], 2.0, -8.0, ALU.mult, ALU.add)
                    v.scalar_tensor_tensor(t[:], r[:], -2.0, t[:], ALU.add, ALU.max)
                    v.scalar_tensor_tensor(t[:], r[:], 0.5, t[:], ALU.mult, ALU.max)
                    v.tensor_tensor(t[:], t[:], sgn[:], ALU.mult)
                    op = v.tensor_tensor(self.dq_o[half][:], t[:],
                                         self.dq_scx[:, 512 * (ch % 2):512 * (ch % 2 + 1)], ALU.mult)
                Ss["dqv"].inc(op, 1)

        @blk.gpsimd
        def _(g):
            def outs(idx):
                m, i, ch = self.dq_units[idx]
                g.wait_ge(Ss["dqv"].h, idx + 1)
                r0 = 2048 * m + 256 * i
                cl = slice(512 * ch, 512 * (ch + 1))
                Ss["dqw"].inc(g.dma_start(self.cinW[r0:r0 + 256:2, cl], self.dq_o[0][:]), 16)
                Ss["dqw"].inc(g.dma_start(self.cinW[r0 + 1:r0 + 256:2, cl], self.dq_o[1][:]), 16)

            for idx in range(lo, hi):
                m, i, ch = self.dq_units[idx]
                if idx > lo:
                    outs(idx - 1)
                if ch % 2 == 0:
                    if idx >= 1:
                        g.wait_ge(Ss["dqv"].h, idx)
                    c0 = 1024 * (ch // 2)
                    for grp in range(8):
                        Ss["ffa"].inc(g.dma_start(
                            self.dq_scx[16 * grp:16 * (grp + 1), :],
                            scs[m][8 * i + grp, None, c0:c0 + 1024].broadcast_to((16, 1024))), 16)
            outs(hi - 1)

        if hi == len(self.dq_units):
            @blk.gpsimd
            def _(g):
                g.wait_ge(Ss["dqw"].h, 32 * len(self.dq_units))
                g.collective_compute(
                    "AllGather", mybir.AluOpType.bypass,
                    replica_groups=[list(range(NCORE))],
                    ins=[self.cinW.ap().opt()],
                    outs=[self.gathW.ap().opt()],
                ).then_inc(Ss["ccw"].h, 1)
                Ss["ccw"].n += 1

    def dq_ffa_target(self, idx):
        tot = self.dq_ffa_base
        for k in range(idx + 1):
            m, i, ch = self.dq_units[k]
            tot += 16 + (128 if ch % 2 == 0 else 0)
        return tot

    def layout_p25(self):
        self.o_ynt = [self.arena[:, 2048 * i:2048 * (i + 1)].bitcast(BF16) for i in range(32)]
        self.o_ow = [self.arena[:, 65536 + 32768 * p:65536 + 32768 * (p + 1)].bitcast(BF16)
                     for p in range(2)]
        self.o_ev = [self.arena[:, 131072 + 2048 * p:131072 + 2048 * (p + 1)].bitcast(F32)
                     for p in range(2)]

    def p25_half(self, blk, H):
        Ss = self.S
        pb = self.pb
        b_ynl = Ss["gtl"].n
        b_owlE = Ss["ffw"].n
        b_owlO = Ss["ffh"].n
        b_ope = Ss["ope"].n
        b_opv = Ss["p2f"].n
        b_opwE = Ss["ffc"].n
        b_opwO = Ss["ffd"].n
        r0 = 1024 * H

        @blk.sync
        def _(sync):
            sync.wait_ge(Ss["opw"].h, 128 * (H + 1))
            if H == 0:
                sync.wait_ge(Ss["p2e"].h, Ss["p2e"].n)
            else:
                sync.wait_ge(Ss["ope"].h, b_ope)
            for hp in range(32):
                Ss["gtl"].inc(sync.dma_start_transpose(
                    self.o_ynt[hp][:, 0:1024], self.yn_tm[r0:r0 + 1024, 128 * hp:128 * (hp + 1)]), 16)
            sync.wait_ge(Ss["opv"].h, Ss["opv"].n)
            for eg in range(4):
                if eg >= 2:
                    sync.wait_ge(Ss["ope"].h, b_ope + 8 * (eg - 1))
                Ss["ffw" if eg % 2 == 0 else "ffh"].inc(sync.dma_start(
                    self.o_ow[eg % 2][:].rearrange("p (t n) -> p t n", t=32),
                    self.ow16[:, 512 * eg:512 * (eg + 1)].rearrange("(t p) n -> p t n", p=128)), 16)

        @blk.tensor
        def _(t):
            t.wait_ge(Ss["opv"].h, Ss["opv"].n)
            if H == 1:
                t.wait_ge(Ss["p2f"].h, b_opv)
            t.wait_ge(Ss["gtl"].h, b_ynl + 16 * 32)
            for eg in range(4):
                t.wait_ge(Ss["ffw" if eg % 2 == 0 else "ffh"].h,
                          (b_owlE if eg % 2 == 0 else b_owlO) + 16 * (eg // 2 + 1))
                for tt in range(8):
                    y = 8 * eg + tt
                    bank = pb[y % 2]
                    if y >= 2:
                        t.wait_ge(Ss["p2f"].h, b_opv + y - 1)
                    mm = None
                    for hp in range(32):
                        mm = t.matmul(bank[:, 0:512],
                                      self.o_ynt[hp][:, 128 * tt:128 * (tt + 1)],
                                      self.o_ow[eg % 2][:, 512 * hp:512 * (hp + 1)],
                                      start=(hp == 0), stop=(hp == 31))
                    Ss["ope"].inc(mm, 1)

        @blk.vector
        def _(v):
            for eg in range(4):
                for tt in range(8):
                    y = 8 * eg + tt
                    bank = pb[y % 2]
                    v.wait_ge(Ss["ope"].h, b_ope + y + 1)
                    if not (H == 0 and y < 2):
                        v.wait_ge(Ss["ffc" if y % 2 == 0 else "ffd"].h,
                                  (b_opwE if y % 2 == 0 else b_opwO) + 16 * (y // 2))
                    op = v.tensor_copy(self.o_ev[y % 2][:], bank[:, 0:512])
                    Ss["p2f"].inc(op, 1)

        @blk.gpsimd
        def _(g):
            for eg in range(4):
                for tt in range(8):
                    y = 8 * eg + tt
                    g.wait_ge(Ss["p2f"].h, b_opv + y + 1)
                    rr = r0 + 128 * tt
                    Ss["ffc" if y % 2 == 0 else "ffd"].inc(g.dma_start(
                        self.out[rr:rr + 128, 512 * eg:512 * (eg + 1)], self.o_ev[y % 2][:]), 16)

    def layout_p3(self):
        a = self.arena
        self.f_xnt = [a[:, 2048 * k:2048 * (k + 1)].bitcast(BF16) for k in range(16)]
        self.f_int = a[:, 32768:65536].bitcast(BF16)
        self.f_wg = [a[:, 65536 + 4096 * p:65536 + 4096 * (p + 1)].bitcast(BF16) for p in range(2)]
        self.f_wd = [a[:, 73728 + 16384 * p:73728 + 16384 * (p + 1)].bitcast(BF16) for p in range(2)]
        self.f_gs = a[:, 106496:108544].bitcast(F32)
        self.f_us = a[:, 108544:110592].bitcast(F32)
        self.f_s2 = a[:, 110592:112640].bitcast(F32)
        self.f_sg = a[:, 112640:114688].bitcast(F32)
        self.f_ev = [a[:, 114688 + 2048 * p:114688 + 2048 * (p + 1)].bitcast(F32) for p in range(2)]
        self.f_ut = a[:, 118784:126976].bitcast(F32)

    def p3(self, blk):
        Ss = self.S
        ffc_oproj = (Ss["ffc"].n, Ss["ffd"].n)
        for TH in range(2):
            self.p3_half(blk, TH, ffc_oproj)


    def p3_half(self, blk, TH, ffc_oproj):
        Ss = self.S
        pb = self.pb
        r0 = 1024 * TH
        b_xl = Ss["gtl"].n
        b_ui = Ss["uld"].n
        b_uacc = Ss["dtv"].n
        ffa_prev = Ss["ffa"].n

        @blk.sync
        def _(sync):
            sync.wait_ge(Ss["ccw"].h, 1)
            sync.wait_ge(Ss["ffc"].h, ffc_oproj[0])
            sync.wait_ge(Ss["ffd"].h, ffc_oproj[1])
            if TH == 1:
                sync.wait_ge(Ss["ffa"].h, ffa_prev)
            for k in range(16):
                Ss["gtl"].inc(sync.dma_start_transpose(
                    self.f_xnt[k][:, 0:1024],
                    self.xn_tm[3 + r0:3 + r0 + 1024, 128 * k:128 * (k + 1)]), 16)
            for i in range(8):
                if i >= 1:
                    sync.wait_ge(Ss["dtv"].h, b_uacc + 16 * i)
                rr = r0 + 128 * i
                Ss["uld"].inc(sync.dma_start(self.f_ut[:, 0:2048], self.u[rr:rr + 128, :]), 16)

        @blk.gpsimd
        def _(g):
            g.wait_ge(Ss["ffc"].h, ffc_oproj[0])
            g.wait_ge(Ss["ffd"].h, ffc_oproj[1])
            for i in range(8):
                g.wait_ge(Ss["uld"].h, b_ui + 16 * (i + 1))
                rr = r0 + 128 * i
                Ss["dtv"].inc(g.dma_start(self.out[rr:rr + 128, :], self.f_ut[:, 0:2048],
                                          accum_op=ALU.add), 16)

        for s in range(8):
            self.p3_slab(blk, TH, s, b_xl + 16 * 16, b_uacc + 16 * 8)

    def p3_slab(self, blk, TH, s, xnt_ready, uacc_done):
        Ss = self.S
        pb = self.pb
        r0 = 1024 * TH
        b_wlE = Ss["ffb"].n
        b_wlO = Ss["ffg"].n
        b_wdE = Ss["ffw"].n
        b_wdO = Ss["ffh"].n
        b_ge = Ss["ffa"].n
        b_gv = Ss["p2d"].n
        b_pa = Ss["p2a"].n
        b_ax = Ss["aex"].n
        b_dn = Ss["ope"].n
        b_dv = Ss["p2f"].n
        b_dwE = Ss["ffc"].n
        b_dwO = Ss["ffd"].n
        grow = 6144 * s
        urow = 6144 * s + 2048
        drow = 6144 * s + 4096

        @blk.sync
        def _(sync):
            sync.wait_ge(Ss["gtl"].h, xnt_ready)
            for nt in range(16):
                for gu, row in ((0, grow), (1, urow)):
                    it = 2 * nt + gu
                    if it >= 2:
                        sync.wait_ge(Ss["ffa"].h, b_ge + 2 * (it - 1))
                    Ss["ffb" if it % 2 == 0 else "ffg"].inc(sync.dma_start(
                        self.f_wg[it % 2][:].rearrange("p (k n) -> p k n", k=16),
                        self.gathW[row:row + 2048, 128 * nt:128 * (nt + 1)].rearrange(
                            "(k p) n -> p k n", p=128)), 16)
            for eg in range(4):
                if eg >= 2:
                    sync.wait_ge(Ss["ope"].h, b_dn + 8 * (eg - 1))
                Ss["ffw" if eg % 2 == 0 else "ffh"].inc(sync.dma_start(
                    self.f_wd[eg % 2][:].rearrange("p (k n) -> p k n", k=16),
                    self.gathW[drow:drow + 2048, 512 * eg:512 * (eg + 1)].rearrange(
                        "(k p) n -> p k n", p=128)), 16)

        @blk.tensor
        def _(t):
            for nt in range(16):
                for gu in range(2):
                    it = 2 * nt + gu
                    t.wait_ge(Ss["ffb" if it % 2 == 0 else "ffg"].h,
                              (b_wlE if it % 2 == 0 else b_wlO) + 16 * (it // 2 + 1))
                    if it == 0 and not (TH == 0 and s == 0):
                        t.wait_ge(Ss["p2d"].h, b_gv - 1)
                    for tg in range(2):
                        q = 4 * nt + 2 * gu + tg
                        bank = pb[q % 4]
                        if nt >= 1:
                            m_prev = 2 * (nt - 1) + tg
                            t.wait_ge(Ss["p2d"].h, b_gv + 3 * m_prev + 1 + gu)
                        mm = None
                        for k in range(16):
                            mm = t.matmul(bank[:, 0:512],
                                          self.f_wg[it % 2][:, 128 * k:128 * (k + 1)],
                                          self.f_xnt[k][:, 512 * tg:512 * (tg + 1)],
                                          start=(k == 0), stop=(k == 15))
                        Ss["ffa"].inc(mm, 1)
            for eg in range(4):
                t.wait_ge(Ss["ffw" if eg % 2 == 0 else "ffh"].h,
                          (b_wdE if eg % 2 == 0 else b_wdO) + 16 * (eg // 2 + 1))
                t.wait_ge(Ss["p2d"].h, b_gv + 96)
                for tt in range(8):
                    y = 8 * eg + tt
                    bank = pb[4 + y % 2]
                    if y >= 2:
                        t.wait_ge(Ss["p2f"].h, b_dv + y - 1)
                    mm = None
                    for nl in range(16):
                        mm = t.matmul(bank[:, 0:512],
                                      self.f_int[:, 1024 * nl + 128 * tt:1024 * nl + 128 * (tt + 1)],
                                      self.f_wd[eg % 2][:, 512 * nl:512 * (nl + 1)],
                                      start=(nl == 0), stop=(nl == 15))
                    Ss["ope"].inc(mm, 1)

        @blk.vector
        def _(v):
            v.wait_ge(Ss["ope"].h, b_dn)
            for nt in range(16):
                for tg in range(2):
                    m = 2 * nt + tg
                    qg = 4 * nt + tg
                    qu = 4 * nt + 2 + tg
                    v.wait_ge(Ss["ffa"].h, b_ge + qg + 1)
                    Ss["p2d"].inc(v.tensor_copy(self.f_gs[:], pb[qg % 4][:, 0:512]), 1)
                    v.wait_ge(Ss["ffa"].h, b_ge + qu + 1)
                    Ss["p2d"].inc(v.tensor_copy(self.f_us[:], pb[qu % 4][:, 0:512]), 1)
                    v.tensor_tensor(self.f_s2[:], self.f_gs[:], self.f_gs[:], ALU.mult)
                    v.tensor_scalar(self.f_s2[:], self.f_s2[:], 0.044715, 1.0, ALU.mult, ALU.add)
                    Ss["p2a"].inc(v.tensor_tensor(self.f_s2[:], self.f_s2[:], self.f_gs[:], ALU.mult), 1)
                    v.wait_ge(Ss["aex"].h, b_ax + m + 1)
                    v.tensor_tensor(self.f_gs[:], self.f_gs[:], self.f_sg[:], ALU.mult)
                    ic = v.tensor_tensor(
                        self.f_int[:, 1024 * nt + 512 * tg:1024 * nt + 512 * (tg + 1)],
                        self.f_gs[:], self.f_us[:], ALU.mult)
                    Ss["p2d"].inc(ic, 1)
            for eg in range(4):
                for tt in range(8):
                    y = 8 * eg + tt
                    bank = pb[4 + y % 2]
                    v.wait_ge(Ss["ope"].h, b_dn + y + 1)
                    if not (TH == 0 and s == 0 and y < 2):
                        v.wait_ge(Ss["ffc" if y % 2 == 0 else "ffd"].h,
                                  (b_dwE if y % 2 == 0 else b_dwO) + 16 * (y // 2))
                    Ss["p2f"].inc(v.tensor_copy(self.f_ev[y % 2][:], bank[:, 0:512]), 1)

        @blk.scalar
        def _(s_):
            for nt in range(16):
                for tg in range(2):
                    m = 2 * nt + tg
                    s_.wait_ge(Ss["p2a"].h, b_pa + m + 1)
                    if m >= 1:
                        s_.wait_ge(Ss["p2d"].h, b_gv + 3 * (m - 1) + 3)
                    Ss["aex"].inc(s_.activation(self.f_sg[:], self.f_s2[:], AF.Sigmoid,
                                                scale=1.5957691216057308), 1)

        @blk.gpsimd
        def _(g):
            if s == 0:
                g.wait_ge(Ss["dtv"].h, uacc_done)
            for eg in range(4):
                for tt in range(8):
                    y = 8 * eg + tt
                    g.wait_ge(Ss["p2f"].h, b_dv + y + 1)
                    if y == 0 and not (TH == 0 and s == 0):
                        g.wait_ge(Ss["ffc"].h, b_dwE)
                        g.wait_ge(Ss["ffd"].h, b_dwO)
                    rr = r0 + 128 * tt
                    Ss["ffc" if y % 2 == 0 else "ffd"].inc(g.dma_start(
                        self.out[rr:rr + 128, 512 * eg:512 * (eg + 1)], self.f_ev[y % 2][:],
                        accum_op=ALU.add), 16)

    def finish_stage(self, blk, wait_list):
        Ss = self.S

        @blk.gpsimd
        def _(g):
            for sem, val in wait_list:
                g.wait_ge(sem.h, val)
            st = Ss["outw"]
            for r in range(16):
                st.inc(g.dma_start(self.out[128 * r:128 * (r + 1), :], self.a_ubuf[:, 0:2048]), 16)
            for t, src in self.tap_list:
                st.inc(g.dma_start(t[:], src), 16)
            g.wait_ge(st.h, st.n)

    def build_full(self):
        self.declare_io()
        self.alloc_sbuf()
        self.alloc_psum()
        self.make_sems()
        self.layout_p01()
        self.layout_p2()
        self.layout_p25()
        self.layout_p3()
        self.conv_sil_done = {}
        with self.nc.Block() as blk:
            self.p0(blk)
            self.p0_ow(blk)
            self.dq_init()
            self.p1_half(blk, 0)
            self.dequant_piece(blk, 24)
            self.p1_half(blk, 1)
            self.dequant_piece(blk, 24)
            self.p2_pre(blk)
            self.dequant_piece(blk, 12)
            for cc in range(NCH):
                self.p2_chunk(blk, cc)
                if cc < 4:
                    self.dequant_piece(blk, 9)
            self.p25_half(blk, 0)
            self.p25_half(blk, 1)
            self.p3(blk)
            self.add_tap_dram("gathW", self.gathW[0:6144, :], [6144, 2048], BF16)
            self.add_tap_dram("attnO", self.attn[:], [T, EMBED], F32)
            self.add_tap_dram("yn", self.yn_tm[:], [T, DIN], BF16)
            Ss = self.S

            @blk.gpsimd
            def _(g):
                g.wait_ge(Ss["ffc"].h, Ss["ffc"].n)
                g.wait_ge(Ss["ffd"].h, Ss["ffd"].n)
                g.wait_ge(Ss["dtv"].h, Ss["dtv"].n)
                st = Ss["outw"]
                for t, srcap in self.tap_list:
                    st.inc(g.dma_start(t[:], srcap), 16)
                g.wait_ge(st.h, st.n)
        return self.nc

    def build_stage3(self):
        self.declare_io()
        self.alloc_sbuf()
        self.alloc_psum()
        self.make_sems()
        self.layout_p01()
        self.layout_p2()
        self.conv_sil_done = {}
        with self.nc.Block() as blk:
            self.p0(blk)
            for h in range(2):
                self.p1_half(blk, h)
            self.p2_pre(blk)
            for cc in range(NCH):
                self.p2_chunk(blk, cc)
            self.add_tap_dram("yn", self.yn_tm[:], [T, DIN], BF16)
            if "prev" in self.taps:
                t = self.nc.dram_tensor("tap_prev", [64, DIN], F32, kind="ExternalOutput")
                self.tap_list.append((t, self.s_loc))
            if "sin" in self.taps:
                pass
            wl = [(self.S[x], self.S[x].n) for x in
                  ["scv", "opw", "p2e", "aex", "p2d", "suw", "zfw", "xfw"]]
            self.finish_stage(blk, wl)
        return self.nc

    def build_stage2(self):
        self.declare_io()
        self.alloc_sbuf()
        self.alloc_psum()
        self.make_sems()
        self.layout_p01()
        self.conv_sil_done = {}
        with self.nc.Block() as blk:
            self.p0(blk)
            for h in range(2):
                self.p1_half(blk, h)
            self.add_tap_dram("xn", self.xn_tm[0:2051, :], [2051, EMBED], BF16)
            self.add_tap_dram("zfm", self.z_fm[:], [DIN, T], BF16)
            self.add_tap_dram("xfm", self.x_fm[:], [DIN, T], BF16)
            self.add_tap_dram("su", self.su_dram[:], [NCH * DSTATE, DIN], F32)
            self.add_tap_dram("aflat", self.aflat[:], [NCH, NH * CHUNK], F32)
            if "sloc" in self.taps:
                t = self.nc.dram_tensor("tap_sloc", [64, DIN], F32, kind="ExternalOutput")
                self.tap_list.append((t, self.s_loc))
            if "dtpos" in self.taps:
                t = self.nc.dram_tensor("tap_dtpos", [32, T], F32, kind="ExternalOutput")
                self.tap_list.append((t, self.k_dtpos))
            if "etm" in self.taps:
                t = self.nc.dram_tensor("tap_etm", [128, NCH * 64], F32, kind="ExternalOutput")
                self.tap_list.append((t, self.k_etm[:]))
            if "atm" in self.taps:
                t = self.nc.dram_tensor("tap_atm", [128, NCH * 64], F32, kind="ExternalOutput")
                self.tap_list.append((t, self.k_atm[:]))
            if "dttm" in self.taps:
                t = self.nc.dram_tensor("tap_dttm", [128, NCH * 64], F32, kind="ExternalOutput")
                self.tap_list.append((t, self.k_dttm[:]))
            wl = [(self.S[x], self.S[x].n) for x in
                  ["scv", "suw", "zfw", "xfw", "aex", "sue"]]
            self.finish_stage(blk, wl)
        return self.nc


def host_prep(inputs, core):
    b, q = core // 4, core % 4
    u_full = np.asarray(inputs["L0_attn_u"])
    u = np.ascontiguousarray(u_full[b, 2048 * q:2048 * (q + 1)], dtype=np.float32)
    if q == 0:
        uhalo = np.zeros((3, EMBED), np.float32)
    else:
        uhalo = np.ascontiguousarray(u_full[b, 2048 * q - 3:2048 * q], dtype=np.float32)

    W = np.asarray(inputs["L0_attn_in_proj_w"]).astype(np.float32)
    Wp = np.zeros((NCT * 128, EMBED), np.float32)
    Wp[0:8352] = W
    tiles = Wp.reshape(NCT, 128, KT, 128).transpose(0, 2, 3, 1)
    ipw = np.ascontiguousarray(tiles.reshape(NCT * KT * 128, 128))

    oww = np.ascontiguousarray(np.asarray(inputs["L0_attn_o_proj_w"]).astype(np.float32).T)

    cw = np.asarray(inputs["L0_attn_conv_w"]).astype(np.float32)
    convw = np.ascontiguousarray(cw.reshape(33, 128, 4).transpose(1, 0, 2).reshape(128, 132))
    cb = np.asarray(inputs["L0_attn_conv_b"]).astype(np.float32)
    convb = np.ascontiguousarray(cb.reshape(33, 128).T)

    alog = np.ascontiguousarray(np.asarray(inputs["L0_attn_A_log"]).astype(np.float32).reshape(NH, 1))
    dtb = np.ascontiguousarray(np.asarray(inputs["L0_attn_dt_bias"]).astype(np.float32).reshape(NH, 1))
    dskip = np.ascontiguousarray(np.asarray(inputs["L0_attn_D_skip"]).astype(np.float32).reshape(1, NH))
    normw = np.ascontiguousarray(np.asarray(inputs["L0_attn_norm_w"]).astype(np.float32))

    constv = np.ascontiguousarray(np.broadcast_to(
        np.array([1e-5, 1e-6, 1.0, 0.5], np.float32), (128, 4)))
    idbf = np.eye(128, dtype=BF)
    idf32 = np.eye(128, dtype=np.float32)
    m = np.zeros((2, 128, 256), np.float32)
    for jt in range(2):
        jl = np.arange(128)[:, None] + jt * 128
        ii = np.arange(256)[None, :]
        m[jt] = np.where(ii >= jl, 0.0, -1e30)
    maskt = np.ascontiguousarray(m.reshape(256, 256))
    ones32 = np.ones((NH, 256), np.float32)
    onesf = np.ones((1, 128), np.float32)

    sel = np.zeros((9, 8), np.float32)
    for j in range(8):
        same = (j // 4 == core // 4) and (j < core)
        if same:
            for mm_ in range(j + 1, core):
                sel[mm_, j] = 1.0
        else:
            sel[8, j] = -1e30
    sel9 = np.ascontiguousarray(sel)

    def packu8(x):
        a = np.asarray(x).astype(np.int32)
        return np.ascontiguousarray(a.view(np.uint8).reshape(a.shape[0], a.shape[1], 4)[:, :, 0])

    pg = packu8(inputs["L0_ffn_gate_W_packed"]).T
    pu = packu8(inputs["L0_ffn_up_W_packed"]).T
    pd = packu8(inputs["L0_ffn_down_W_packed"]).T
    n0 = 2048 * core
    pkg = np.ascontiguousarray(pg[:, n0:n0 + 2048])
    pku = np.ascontiguousarray(pu[:, n0:n0 + 2048])
    pkd = np.ascontiguousarray(pd[1024 * core:1024 * (core + 1), :])
    sg = np.asarray(inputs["L0_ffn_gate_scales"]).astype(np.float32).T
    su_ = np.asarray(inputs["L0_ffn_up_scales"]).astype(np.float32).T
    sd_ = np.asarray(inputs["L0_ffn_down_scales"]).astype(np.float32).T
    scg = np.ascontiguousarray(sg[:, n0:n0 + 2048])
    scu = np.ascontiguousarray(su_[:, n0:n0 + 2048])
    scd = np.ascontiguousarray(sd_[64 * core:64 * (core + 1), :])

    return dict(u=u, uh=uhalo, ipw=ipw, oww=oww, convw=convw, convb=convb,
                alog=alog, dtb=dtb, dskip=dskip, normw=normw, idbf=idbf,
                idf32=idf32, maskt=maskt, ones32=ones32, onesf=onesf, sel9=sel9, constv=constv,
                pkg=pkg, pku=pku, pkd=pkd, scg=scg, scu=scu, scd=scd)


from concourse.bass_utils import run_bass_kernel_spmd


def kernel(**inputs):
    b = Build()
    nc = b.build_full()
    ims = [host_prep(inputs, c) for c in range(NCORE)]
    res = run_bass_kernel_spmd(nc, ims, core_ids=list(range(NCORE)))
    out = np.zeros((2, 8192, EMBED), np.float32)
    for c in range(NCORE):
        out[c // 4, 2048 * (c % 4):2048 * (c % 4 + 1)] = res.results[c]["outp"]
    return out
